# version 08
# speedup vs baseline: 1.3633x; 1.0128x over previous
import math
import numpy as np
import ml_dtypes
import concourse.bass as bass
import concourse.mybir as mybir
from concourse.bass_utils import run_bass_kernel_spmd
from contextlib import ExitStack

F32 = mybir.dt.float32
BF16 = mybir.dt.bfloat16
AF = mybir.ActivationFunctionType
ALU = mybir.AluOpType
AX = mybir.AxisListType
BF16NP = ml_dtypes.bfloat16

DEBUG_L2 = False
DEBUG_L3B = False
L5_MAXPHASE = 4
DEBUG_L5 = False
L7_MAXPHASE = 3
L7_NU = 8
L7_SUB = 3
B = 2
S = 8192
D = 1024
NCH = D // 128
NCORE = 8
EPS0 = 1e-5
EPS1 = 1e-6
H0, E0 = 8, 128
H1, E1, CH1 = 16, 64, 128


def _derived():
    global TOK, SBLK, GRP, NG
    TOK = B * S // NCORE
    SBLK = S // 4
    GRP = min(512, TOK)
    NG = TOK // GRP


_derived()


def _set_size(s):
    global S
    S = s
    _derived()



def _fmslice(x, c):
    b, blk = c // 4, c % 4
    return np.ascontiguousarray(x[b, blk * SBLK:(blk + 1) * SBLK, :].T)


def _rows(p):
    return np.ascontiguousarray(np.asarray(p, np.float32).reshape(NCH, 128))



class Ctx:
    def __init__(self, name):
        self.nc = bass.Bass("TRN2", target_bir_lowering=False, debug=False)
        self.es = ExitStack()
        self._sems = {}
        self._cnt = {}

    def sem(self, name):
        if name not in self._sems:
            self._sems[name] = self.es.enter_context(self.nc.semaphore(name))
            self._cnt[name] = 0
        return self._sems[name]

    def inc(self, instr, name, k=1):
        s = self.sem(name)
        instr.then_inc(s, k)
        self._cnt[name] += k

    def cnt(self, name):
        self.sem(name)
        return self._cnt[name]

    def wait(self, eng, name, val=None):
        eng.wait_ge(self.sem(name), self._cnt[name] if val is None else val)

    def sb(self, name, shape, dt=F32):
        return self.es.enter_context(self.nc.sbuf_tensor(name, list(shape), dt))

    def ps(self, name, shape, dt=F32):
        return self.es.enter_context(self.nc.psum_tensor(name, list(shape), dt))

    def din(self, name, shape, dt=F32):
        return self.nc.dram_tensor(name, list(shape), dt, kind="ExternalInput").ap()

    def dout(self, name, shape, dt=F32):
        return self.nc.dram_tensor(name, list(shape), dt, kind="ExternalOutput").ap()

    def vtick(self):
        v = self.nc.vector
        self.inc(v.engine_nop(), "_vv", 1)
        self.wait(v, "_vv")



def emit_fm_norm(cx, src, out, nchunks, width, eps, sc, pre, ready):
    nc = cx.nc
    v, s, t = nc.vector, nc.scalar, nc.tensor
    LB = {nm: cx.cnt("ln" + nm) for nm in ("sq", "sqrd", "var", "std", "rw", "bc", "ap")}
    for e in (v, s, t):
        for nm, val in ready:
            cx.wait(e, nm, val)
    sq0, sq1 = sc["sq0"], sc["sq1"]
    sqb = [sq0, sq1]
    for i in range(nchunks):
        if i >= 2:
            cx.wait(s, "lnsqrd", LB["sqrd"] + i - 1)
        ins = s.activation(out=sqb[i % 2][:], in_=src[i], func=AF.Square)
        cx.inc(ins, "lnsq")
        cx.wait(t, "lnsq", LB["sq"] + i + 1)
        t.matmul(sc["psx"][:], sc["ones_col"][:], src[i], start=(i == 0), stop=(i == nchunks - 1))
        ins = t.matmul(sc["pss"][:], sc["ones_col"][:], sqb[i % 2][:], start=(i == 0), stop=(i == nchunks - 1))
        cx.inc(ins, "lnsqrd")
    ntot = float(nchunks * 128)
    cx.wait(v, "lnsqrd", LB["sqrd"] + nchunks)
    ins = v.tensor_scalar_mul(sc["r_mu"][:], sc["psx"][:], 1.0 / ntot)
    cx.inc(ins, "_vv")
    ins = v.tensor_scalar_mul(sc["r_ss"][:], sc["pss"][:], 1.0 / ntot)
    cx.inc(ins, "_vv")
    cx.wait(v, "_vv")
    ins = v.tensor_mul(sc["r_t"][:], sc["r_mu"][:], sc["r_mu"][:])
    cx.inc(ins, "_vv")
    cx.wait(v, "_vv")
    ins = v.tensor_sub(sc["r_var"][:], sc["r_ss"][:], sc["r_t"][:])
    cx.inc(ins, "lnvar")
    cx.wait(s, "lnvar")
    ins = s.activation(out=sc["r_std"][:], in_=sc["r_var"][:], func=AF.Sqrt, bias=sc["eps_ap"][eps][:])
    cx.inc(ins, "lnstd")
    cx.wait(v, "lnstd")
    ins = v.reciprocal(sc["r_rstd"][:], sc["r_std"][:])
    cx.inc(ins, "_vv")
    cx.wait(v, "_vv")
    ins = v.tensor_mul(sc["r_mursd"][:], sc["r_mu"][:], sc["r_rstd"][:])
    cx.inc(ins, "lnrw")
    srow = sc.get("srow")
    for i in range(nchunks):
        cx.wait(t, "lnrw")
        if i > 0:
            cx.wait(t, "lnap", LB["ap"] + i)
        if srow is None:
            t.matmul(sc["pt1"][:], sc["onesrow"][:], sc["r_rstd"][:], start=True, stop=True)
            ins = t.matmul(sc["pt2"][:], sc["negrow"][:], sc["r_mursd"][:], start=True, stop=True)
        else:
            t.matmul(sc["pt1"][:], srow[i], sc["r_rstd"][:], start=True, stop=True)
            ins = t.matmul(sc["pt2"][:], sc["negsrow"][i], sc["r_mursd"][:], start=True, stop=True)
        cx.inc(ins, "lnbc")
        cx.wait(v, "lnbc", LB["bc"] + i + 1)
        ins = v.tensor_mul(sc["ta"][:], src[i], sc["pt1"][:])
        cx.inc(ins, "_vv")
        cx.wait(v, "_vv")
        bcol = sc.get("bcol")
        if bcol is None:
            ins = v.tensor_add(out[i], sc["ta"][:], sc["pt2"][:])
        else:
            ins = v.scalar_tensor_tensor(out[i], sc["ta"][:], bcol[i], sc["pt2"][:], ALU.add, ALU.add)
        cx.inc(ins, "lnap")
    return ("lnap", LB["ap"])


def _ln_scratch(cx, width, name="lnS", ps_map=None, need_scale=True):
    d = {}
    d["sq0"] = cx.sb(name + "_sq0", [128, width], F32)
    d["sq1"] = cx.sb(name + "_sq1", [128, width], F32)
    d["ta"] = cx.sb(name + "_ta", [128, width], F32)
    d["ones_col"] = cx.sb(name + "_ones", [128, 1], F32)
    d["onesrow"] = cx.sb(name + "_onesr", [1, 128], F32)
    d["negrow"] = cx.sb(name + "_negr", [1, 128], F32)
    if ps_map is None:
        d["psx"] = cx.ps(name + "_psx", [1, width], F32)
        d["pss"] = cx.ps(name + "_pss", [1, width], F32)
        d["pt1"] = cx.ps(name + "_pt1", [128, width], F32)
        d["pt2"] = cx.ps(name + "_pt2", [128, width], F32)
    else:
        d.update(ps_map)
    for nm in ("r_mu", "r_ss", "r_t", "r_var", "r_std", "r_rstd"):
        d[nm] = cx.sb(name + "_" + nm, [1, width], F32)
    d["r_mursd"] = cx.sb(name + "_rmursd", [1, width], F32)
    d["eps_ap"] = {e: cx.sb(name + f"_eps{i}", [1, 1], F32) for i, e in enumerate((EPS0, EPS1))}
    if need_scale:
        d["lhs2"] = cx.sb(name + "_lhs2", [1, 128 * NCH], F32)
    return d


def _ln_scratch_init(cx, d):
    g = cx.nc.gpsimd
    g.memset(d["ones_col"][:], 1.0)
    g.memset(d["onesrow"][:], 1.0)
    g.memset(d["negrow"][:], -1.0)
    for e, buf in d["eps_ap"].items():
        ins = g.memset(buf[:], float(e))
    cx.inc(ins, "consts")



def build_L1():
    cx = Ctx("L1")
    nc = cx.nc
    x_in = cx.din("x_fm", [D, TOK])
    ln1s = cx.din("ln1s", [NCH, 128])
    ln1b = cx.din("ln1b", [NCH, 128])
    wg = cx.din("wg0", [D, D])
    o_xn0 = cx.dout("xn0", [D, TOK])
    o_xni = cx.dout("xni0", [D, TOK], BF16)
    o_sw = cx.dout("swish0", [D, TOK])
    with cx.es:
        cx.es.enter_context(nc.Block())
        v, s, t, g, sy = nc.vector, nc.scalar, nc.tensor, nc.gpsimd, nc.sync
        xg = [cx.sb(f"xg{i}", [128, GRP], F32) for i in range(NCH)]
        xn0g = [cx.sb(f"xn0g{i}", [128, GRP], F32) for i in range(NCH)]
        xnib = [cx.sb(f"xnib{i}", [128, GRP], BF16) for i in range(NCH)]
        swg = cx.sb("swg", [128, GRP], F32)
        wgb = [cx.sb(f"wgb{i}", [128, D], BF16) for i in range(NCH)]
        lnS = _ln_scratch(cx, GRP, "lnS")
        lnS["srow"] = None
        psg = cx.ps("psg", [128, GRP], F32)
        srow = [cx.sb(f"srow{i}", [1, 128], F32) for i in range(NCH)]

        for i in range(NCH):
            ins = g.dma_start(out=wgb[i][:], in_=wg[128 * i:128 * (i + 1), :])
            cx.inc(ins, "wld", 16)
        for i in range(NCH):
            ins = sy.dma_start(out=srow[i][:], in_=ln1s[i:i + 1, :])
            cx.inc(ins, "cld", 16)
        bcol = [cx.sb(f"bcol{i}", [128, 1], F32) for i in range(NCH)]
        for i in range(NCH):
            ins = sy.dma_start(out=bcol[i][:], in_=ln1b[i:i + 1, :].rearrange("a k -> (a k)")[:, None])
            cx.inc(ins, "cld", 16)
        _ln_scratch_init(cx, lnS)
        cx.wait(v, "cld")
        for i in range(NCH):
            ins = v.tensor_scalar_mul(lnS["lhs2"][0:1, 128 * i:128 * (i + 1)], srow[i][:], -1.0)
        cx.inc(ins, "negs")

        for gi in range(NG):
            gofs = gi * GRP
            pre = f"g{gi}"
            for i in range(NCH):
                ins = sy.dma_start(out=xg[i][:], in_=x_in[128 * i:128 * (i + 1), gofs:gofs + GRP])
                cx.inc(ins, "xld", 16)
            apA = emit_fm_norm(cx, [xg[i][:] for i in range(NCH)], [xn0g[i][:] for i in range(NCH)],
                         NCH, GRP, EPS0, dict(lnS, srow=None), pre + "a",
                         [("xld", cx.cnt("xld")), ("consts", cx.cnt("consts"))])
            for i in range(NCH):
                cx.wait(sy, apA[0], apA[1] + i + 1)
                ins = sy.dma_start(out=o_xn0[128 * i:128 * (i + 1), gofs:gofs + GRP], in_=xn0g[i][:])
                cx.inc(ins, "st", 16)
            apB = emit_fm_norm(cx, [xn0g[i][:] for i in range(NCH)], [xnib[i][:] for i in range(NCH)],
                         NCH, GRP, EPS1,
                         dict(lnS, srow=[r[:] for r in srow], bcol=[b[:] for b in bcol],
                              negsrow=[lnS["lhs2"][0:1, 128 * i:128 * (i + 1)] for i in range(NCH)]),
                         pre + "b",
                         [(apA[0], apA[1] + NCH), ("negs", cx.cnt("negs")), ("cld", cx.cnt("cld"))])
            for i in range(NCH):
                cx.wait(sy, apB[0], apB[1] + i + 1)
                ins = sy.dma_start(out=o_xni[128 * i:128 * (i + 1), gofs:gofs + GRP], in_=xnib[i][:])
                cx.inc(ins, "st", 16)
            cx.wait(t, "wld")
            cx.wait(t, apB[0], apB[1] + NCH)
            for n in range(NCH):
                if n > 0:
                    cx.wait(t, pre + "sw", n)
                for i in range(NCH):
                    ins = t.matmul(psg[:], wgb[i][:, 128 * n:128 * (n + 1)], xnib[i][:],
                                   start=(i == 0), stop=(i == NCH - 1))
                cx.inc(ins, pre + "gmm")
                cx.wait(s, pre + "gmm", n + 1)
                if n > 0:
                    cx.wait(s, "st", cx.cnt("st"))
                ins = s.activation(out=swg[:], in_=psg[:], func=AF.Silu)
                cx.inc(ins, pre + "sw")
                cx.wait(sy, pre + "sw", n + 1)
                ins = sy.dma_start(out=o_sw[128 * n:128 * (n + 1), gofs:gofs + GRP], in_=swg[:])
                cx.inc(ins, "st", 16)
            for e in (v, s, t, g, sy):
                cx.wait(e, "st")
    return nc


def host_L1(inputs, run=run_bass_kernel_spmd):
    x = np.asarray(inputs["L0_attn_x"])
    nc = build_L1()
    in_maps = []
    for c in range(NCORE):
        in_maps.append({
            "x_fm": _fmslice(x, c),
            "ln1s": _rows(inputs["L0_attn_ln1_scale"][0]),
            "ln1b": _rows(inputs["L0_attn_ln1_bias"][0]),
            "wg0": np.ascontiguousarray(np.asarray(inputs["L0_attn_Wg"][0], np.float32)),
        })
    res = run(nc, in_maps, core_ids=list(range(NCORE)))
    return res.results





def _gammas():
    return 1.0 - 2.0 ** (-5.0 - np.arange(H0, dtype=np.float64))


def xpos_pattern_consts(heads):
    j = np.arange(64, dtype=np.float64)
    base = (2 * j + 0.4 * 128) / (1.4 * 128)
    pos = np.arange(S, dtype=np.float64)
    scale = base[None, :] ** (pos[:, None] / 512.0)
    inv = 1.0 / 10000.0 ** (j / 64.0)
    sinu = pos[:, None] * inv[None, :]
    sn, cs = np.sin(sinu), np.cos(sinu)
    gam = _gammas()
    out = []
    for h in heads:
        qf = gam[h] ** (pos % 512.0)
        kf = gam[h] ** (-(pos % 128.0))
        cq = np.empty((S, 128)); sq = np.empty((S, 128))
        ck = np.empty((S, 128)); sk = np.empty((S, 128))
        cq[:, 0::2] = cs * scale * qf[:, None]; cq[:, 1::2] = cq[:, 0::2]
        sq[:, 0::2] = -sn * scale * qf[:, None]; sq[:, 1::2] = sn * scale * qf[:, None]
        ck[:, 0::2] = cs / scale * kf[:, None]; ck[:, 1::2] = ck[:, 0::2]
        sk[:, 0::2] = -sn / scale * kf[:, None]; sk[:, 1::2] = sn / scale * kf[:, None]
        out.append([np.ascontiguousarray(a.T, dtype=np.float32) for a in (cq, sq, ck, sk)])
    return out


def dscale_consts(heads):
    M = S // 128 + 3
    gam = _gammas()
    out = np.zeros((128, len(heads) * M), np.float32)
    for i, h in enumerate(heads):
        for m in range(M):
            out[:, i * M + m] = np.float32(np.float64(gam[h]) ** (128.0 * (m - 3)))
    return out


def diag_masks():
    t = np.arange(128)[:, None]
    q = np.arange(512)[None, :]
    return np.stack([(t <= q - 128 * m).astype(np.float32) for m in range(4)])


def col_swap(w):
    w = np.asarray(w)
    out = np.empty_like(w)
    out[..., 0::2] = w[..., 1::2]
    out[..., 1::2] = w[..., 0::2]
    return out


def build_L2():
    cx = Ctx("L2")
    nc = cx.nc
    NQG = S // 512
    NTT = S // 128
    Mds = S // 128 + 3
    xnb_in = cx.din("xnb", [D, S], BF16)
    wq_in = cx.din("wq", [2, D, 128])
    wqs_in = cx.din("wqs", [2, D, 128])
    wk_in = cx.din("wk", [2, D, 128])
    wks_in = cx.din("wks", [2, D, 128])
    wv_in = cx.din("wv", [2, D, 128])
    pat_in = cx.din("pats", [2, 4, 128, S])
    ds_in = cx.din("dscale", [128, 2 * Mds])
    msk_in = cx.din("dmask", [4, 128, 512])
    gns_in = cx.din("gns", [2, 128])
    gnb_in = cx.din("gnb", [2, 128])
    o_y = cx.dout("y", [2, 128, S])
    dbg = {}
    if DEBUG_L2:
        for nm in ("dqx", "dkx", "dvt"):
            dbg[nm] = cx.dout(nm, [128, S], BF16)
        dbg["dot"] = cx.dout("dot", [128, S])
    with cx.es:
        cx.es.enter_context(nc.Block())
        v, s, t, g, sy = nc.vector, nc.scalar, nc.tensor, nc.gpsimd, nc.sync
        QX = [cx.sb(f"QX{p}", [128, S], BF16) for p in range(2)]
        KX = [cx.sb(f"KX{p}", [128, S], BF16) for p in range(2)]
        VT = [cx.sb(f"VT{p}", [128, S], BF16) for p in range(2)]
        wts = {}
        for nm, src_ap in (("wq", wq_in), ("wqs", wqs_in), ("wk", wk_in), ("wks", wks_in), ("wv", wv_in)):
            wts[nm] = [cx.sb(f"{nm}{p}", [128, NCH * 128], BF16) for p in range(2)]
        xg = [cx.sb(f"xg_{i}", [128, 512], BF16) for i in range(NCH)]
        patb = [cx.sb(f"pat{k}", [128, 512], F32) for k in range(4)]
        dscol = cx.sb("dscol", [128, Mds * 2], F32)
        maskb = [cx.sb(f"mask{m}", [128, 512], F32) for m in range(4)]
        scbf = [cx.sb(f"scbf{j}", [128, 512], BF16) for j in range(2)]
        oT_sb = cx.sb("oT_sb", [128, 512], F32)
        yout = cx.sb("yout", [128, 512], F32)
        gnsr = [cx.sb(f"gnsr{p}", [1, 128], F32) for p in range(2)]
        gnsn = [cx.sb(f"gnsn{p}", [1, 128], F32) for p in range(2)]
        gnbc = [cx.sb(f"gnbc{p}", [128, 1], F32) for p in range(2)]
        ca = cx.sb("ca", [128, 512], F32)
        cb = cx.sb("cb", [128, 512], F32)
        pA = [cx.ps(f"pA{i}", [128, 512], F32) for i in range(6)]
        lnp = {"psx": cx.ps("lnpsx", [1, 512], F32), "pss": cx.ps("lnpss", [1, 512], F32)}
        lnS = _ln_scratch(cx, 512, "lnS", ps_map={"psx": lnp["psx"], "pss": lnp["pss"],
                                                  "pt1": pA[3], "pt2": pA[4]})

        for p in range(2):
            for nm, src_ap in (("wq", wq_in), ("wqs", wqs_in), ("wk", wk_in), ("wks", wks_in), ("wv", wv_in)):
                for i in range(NCH):
                    ins = g.dma_start(out=wts[nm][p][:, 128 * i:128 * (i + 1)],
                                      in_=src_ap[p, 128 * i:128 * (i + 1), :])
                    cx.inc(ins, "wld", 16)
            ins = sy.dma_start(out=gnsr[p][:], in_=gns_in[p:p + 1, :])
            cx.inc(ins, "cld", 16)
            ins = sy.dma_start(out=gnbc[p][:], in_=gnb_in[p:p + 1, :].rearrange("a k -> (a k)")[:, None])
            cx.inc(ins, "cld", 16)
        for m in range(4):
            ins = sy.dma_start(out=maskb[m][:], in_=msk_in[m])
            cx.inc(ins, "cld", 16)
        ins = sy.dma_start(out=dscol[:], in_=ds_in[:])
        cx.inc(ins, "cld", 16)
        _ln_scratch_init(cx, lnS)
        cx.wait(nc.vector, "cld")
        for p in range(2):
            ins = nc.vector.tensor_scalar_mul(gnsn[p][:], gnsr[p][:], -1.0)
        cx.inc(ins, "negs")

        cx.wait(t, "wld")
        for gi in range(NQG):
            so = gi * 512
            for i in range(NCH):
                ins = sy.dma_start(out=xg[i][:], in_=xnb_in[128 * i:128 * (i + 1), so:so + 512])
                cx.inc(ins, "xld", 16)
            cx.wait(t, "xld")
            for p in range(2):
                for w_i, nm in enumerate(("wq", "wqs", "wk", "wks")):
                    for i in range(NCH):
                        ins = t.matmul(pA[w_i][:], wts[nm][p][:, 128 * i:128 * (i + 1)], xg[i][:],
                                       start=(i == 0), stop=(i == NCH - 1))
                cx.inc(ins, "pmm")
                for tt in range(4):
                    for i in range(NCH):
                        ins = t.matmul(pA[4][:, 128 * tt:128 * (tt + 1)],
                                       xg[i][:, 128 * tt:128 * (tt + 1)],
                                       wts["wv"][p][:, 128 * i:128 * (i + 1)],
                                       start=(i == 0), stop=(i == NCH - 1))
                cx.inc(ins, "vmm")
                for k in range(4):
                    ins = sy.dma_start(out=patb[k][:], in_=pat_in[p, k, :, so:so + 512])
                    cx.inc(ins, "pld", 16)
                cx.wait(v, "pld")
                cx.wait(v, "pmm")
                ins = v.tensor_mul(ca[:], pA[0][:], patb[0][:])
                cx.inc(ins, "_vv")
                ins = v.tensor_mul(cb[:], pA[1][:], patb[1][:])
                cx.inc(ins, "_vv")
                cx.wait(v, "_vv")
                ins = v.tensor_add(QX[p][:, so:so + 512], ca[:], cb[:])
                cx.inc(ins, "qxd")
                ins = v.tensor_mul(ca[:], pA[2][:], patb[2][:])
                cx.inc(ins, "_vv")
                ins = v.tensor_mul(cb[:], pA[3][:], patb[3][:])
                cx.inc(ins, "_vv")
                cx.wait(v, "_vv")
                ins = v.tensor_add(KX[p][:, so:so + 512], ca[:], cb[:])
                cx.inc(ins, "qxd")
                cx.wait(s, "vmm")
                ins = s.copy(VT[p][:, so:so + 512], pA[4][:])
                cx.inc(ins, "vev")
                cx.wait(t, "qxd")
                cx.wait(t, "vev")
                cx.wait(sy, "qxd")
                cx.wait(sy, "vev")
        if DEBUG_L2:
            for nm, buf in (("dqx", QX[0]), ("dkx", KX[0]), ("dvt", VT[0])):
                cx.wait(sy, "qxd")
                cx.wait(sy, "vev")
                ins = sy.dma_start(out=dbg[nm], in_=buf[:])
                cx.inc(ins, "st", 16)
        nm2 = 0
        nev = 0
        nsc = 0
        for p in range(2):
            for gi in range(NQG):
                so = gi * 512
                Jmax = 4 * gi + 3
                for J in range(Jmax + 2):
                    bi = J % 2
                    if J <= Jmax:
                        if nsc >= 2:
                            cx.wait(t, "ev", nev - 1)
                        ins = t.matmul(pA[bi][:], KX[p][:, 128 * J:128 * (J + 1)],
                                       QX[p][:, so:so + 512], start=True, stop=True)
                        cx.inc(ins, "sc")
                        nsc += 1
                        mprime = J - 4 * gi
                        mcol = dscol[:, p * Mds + (4 * gi - J + 3):p * Mds + (4 * gi - J + 3) + 1]
                        if mprime >= 0:
                            cx.wait(v, "sc", nsc)
                            if nm2 >= 2:
                                cx.wait(v, "m2", nm2 - 1)
                            ins = v.scalar_tensor_tensor(scbf[bi][:], pA[bi][:], mcol,
                                                         maskb[mprime][:], ALU.mult, ALU.mult)
                            cx.inc(ins, "ev")
                        else:
                            cx.wait(s, "sc", nsc)
                            if nm2 >= 2:
                                cx.wait(s, "m2", nm2 - 1)
                            ins = s.activation(out=scbf[bi][:], in_=pA[bi][:], func=AF.Copy, scale=mcol)
                            cx.inc(ins, "ev")
                        nev += 1
                    if J >= 1:
                        Jp = J - 1
                        cx.wait(t, "ev", nev if J > Jmax else nev - 1)
                        ins = t.matmul(pA[2][:], VT[p][:, 128 * Jp:128 * (Jp + 1)], scbf[Jp % 2][:],
                                       start=(Jp == 0), stop=(Jp == Jmax))
                        cx.inc(ins, "m2")
                        nm2 += 1
                cx.wait(s, "m2", nm2)
                ins = s.copy(oT_sb[:], pA[2][:])
                cx.inc(ins, "oev")
                if DEBUG_L2 and p == 0:
                    cx.wait(sy, "oev")
                    ins = sy.dma_start(out=dbg["dot"][:, so:so + 512], in_=oT_sb[:])
                    cx.inc(ins, "st", 16)
                apG = emit_fm_norm(cx, [oT_sb[:]], [yout[:]], 1, 512, EPS1,
                             dict(lnS, srow=[gnsr[p][:]], bcol=[gnbc[p][:]], negsrow=[gnsn[p][:]]),
                             f"gn{p}_{gi}", [("oev", cx.cnt("oev")), ("consts", cx.cnt("consts")),
                                             ("cld", cx.cnt("cld")), ("negs", cx.cnt("negs"))])
                cx.wait(sy, apG[0], apG[1] + 1)
                ins = sy.dma_start(out=o_y[p, :, so:so + 512], in_=yout[:])
                cx.inc(ins, "st", 16)
                for e in (v, s, t):
                    cx.wait(e, "st")
        cx.wait(sy, "st")
    return nc


def host_L2(inputs, layer, xnb_by_batch, run=run_bass_kernel_spmd):
    nc = build_L2()
    in_maps = []
    for c in range(NCORE):
        b = c // 4
        heads = [2 * (c % 4), 2 * (c % 4) + 1]
        pats = xpos_pattern_consts(heads)
        wq = np.asarray(inputs["L0_attn_Wq"][layer], np.float32)
        wk = np.asarray(inputs["L0_attn_Wk"][layer], np.float32)
        wv = np.asarray(inputs["L0_attn_Wv"][layer], np.float32)
        gns = np.asarray(inputs["L0_attn_gn_scale"][layer], np.float32).reshape(H0, 128)
        gnb = np.asarray(inputs["L0_attn_gn_bias"][layer], np.float32).reshape(H0, 128)
        in_maps.append({
            "xnb": xnb_by_batch[b],
            "wq": np.ascontiguousarray(wq[heads]),
            "wqs": np.ascontiguousarray(col_swap(wq[heads])),
            "wk": np.ascontiguousarray(wk[heads]),
            "wks": np.ascontiguousarray(col_swap(wk[heads])),
            "wv": np.ascontiguousarray(wv[heads]),
            "pats": np.ascontiguousarray(np.stack([np.stack(p) for p in pats])),
            "dscale": dscale_consts(heads),
            "dmask": diag_masks(),
            "gns": np.ascontiguousarray(gns[heads]),
            "gnb": np.ascontiguousarray(gnb[heads]),
        })
    res = run(nc, in_maps, core_ids=list(range(NCORE)))
    return res.results





def build_L3a():
    cx = Ctx("L3a")
    nc = cx.nc
    y_in = cx.din("y_fm", [D, TOK])
    sw_in = cx.din("sw_fm", [D, TOK])
    xp_in = cx.din("xp_fm", [D, TOK])
    wo_in = cx.din("wo", [D, D])
    ln2s = cx.din("ln2s", [NCH, 128])
    ln2b = cx.din("ln2b", [NCH, 128])
    o_x1 = cx.dout("x1", [D, TOK])
    o_xn2 = cx.dout("xn2", [D, TOK], BF16)
    with cx.es:
        cx.es.enter_context(nc.Block())
        v, s, t, g, sy = nc.vector, nc.scalar, nc.tensor, nc.gpsimd, nc.sync
        yg = cx.sb("yg", [128, GRP], F32)
        swgc = cx.sb("swgc", [128, GRP], F32)
        xg = [cx.sb(f"xg{i}", [128, GRP], F32) for i in range(NCH)]
        ug = [cx.sb(f"ug{i}", [128, GRP], BF16) for i in range(NCH)]
        x1g = [cx.sb(f"x1g{i}", [128, GRP], F32) for i in range(NCH)]
        xn2b = [cx.sb(f"xn2b{i}", [128, GRP], BF16) for i in range(NCH)]
        wob = [cx.sb(f"wob{i}", [128, D], BF16) for i in range(NCH)]
        srow = [cx.sb(f"srow{i}", [1, 128], F32) for i in range(NCH)]
        bcol = [cx.sb(f"bcol{i}", [128, 1], F32) for i in range(NCH)]
        lnS = _ln_scratch(cx, GRP, "lnS")
        pso = cx.ps("pso", [128, GRP], F32)

        for i in range(NCH):
            ins = g.dma_start(out=wob[i][:], in_=wo_in[128 * i:128 * (i + 1), :])
            cx.inc(ins, "wld", 16)
        for i in range(NCH):
            ins = sy.dma_start(out=srow[i][:], in_=ln2s[i:i + 1, :])
            cx.inc(ins, "cld", 16)
            ins = sy.dma_start(out=bcol[i][:], in_=ln2b[i:i + 1, :].rearrange("a k -> (a k)")[:, None])
            cx.inc(ins, "cld", 16)
        _ln_scratch_init(cx, lnS)
        cx.wait(v, "cld")
        for i in range(NCH):
            ins = v.tensor_scalar_mul(lnS["lhs2"][0:1, 128 * i:128 * (i + 1)], srow[i][:], -1.0)
        cx.inc(ins, "negs")
        negsrow = [lnS["lhs2"][0:1, 128 * i:128 * (i + 1)] for i in range(NCH)]

        for gi in range(NG):
            go = gi * GRP
            pre = f"g{gi}"
            for i in range(NCH):
                ins = sy.dma_start(out=yg[:], in_=y_in[128 * i:128 * (i + 1), go:go + GRP])
                cx.inc(ins, "ld", 16)
                ins = sy.dma_start(out=swgc[:], in_=sw_in[128 * i:128 * (i + 1), go:go + GRP])
                cx.inc(ins, "ld", 16)
                ins = sy.dma_start(out=xg[i][:], in_=xp_in[128 * i:128 * (i + 1), go:go + GRP])
                cx.inc(ins, "ld", 16)
                cx.wait(v, "ld")
                ins = v.tensor_mul(ug[i][:], swgc[:], yg[:])
                cx.inc(ins, pre + "u")
                cx.wait(sy, pre + "u")
            cx.wait(t, "wld")
            cx.wait(t, pre + "u", NCH)
            for n in range(NCH):
                if n > 0:
                    cx.wait(t, pre + "x1", n)
                for i in range(NCH):
                    ins = t.matmul(pso[:], wob[i][:, 128 * n:128 * (n + 1)], ug[i][:],
                                   start=(i == 0), stop=(i == NCH - 1))
                cx.inc(ins, pre + "mm")
                cx.wait(v, pre + "mm", n + 1)
                ins = v.tensor_add(x1g[n][:], xg[n][:], pso[:])
                cx.inc(ins, pre + "x1")
            for i in range(NCH):
                cx.wait(sy, pre + "x1", i + 1)
                ins = sy.dma_start(out=o_x1[128 * i:128 * (i + 1), go:go + GRP], in_=x1g[i][:])
                cx.inc(ins, "st", 16)
            apN = emit_fm_norm(cx, [x1g[i][:] for i in range(NCH)], [xn2b[i][:] for i in range(NCH)],
                         NCH, GRP, EPS1,
                         dict(lnS, srow=[r[:] for r in srow], bcol=[b[:] for b in bcol],
                              negsrow=negsrow),
                         pre + "n",
                         [(pre + "x1", NCH), ("negs", cx.cnt("negs")), ("cld", cx.cnt("cld")),
                          ("consts", cx.cnt("consts"))])
            for i in range(NCH):
                cx.wait(sy, apN[0], apN[1] + i + 1)
                ins = sy.dma_start(out=o_xn2[128 * i:128 * (i + 1), go:go + GRP], in_=xn2b[i][:])
                cx.inc(ins, "st", 16)
            for e in (v, s, t, sy):
                cx.wait(e, "st")
        cx.wait(sy, "st")
    return nc



GELU_C = 0.7978845608028654
GELU_K2 = 0.5 / GELU_C


def build_L3b(last):
    cx = Ctx("L3b")
    nc = cx.nc
    NH = 4096 // 128
    xn2_in = cx.din("xn2", [D, TOK], BF16)
    x1_in = cx.din("x1", [D, TOK])
    w1_in = cx.din("w1", [D, 4096])
    w2_in = cx.din("w2", [4096, D])
    if not last:
        ln1s = cx.din("ln1s", [NCH, 128])
        ln1b = cx.din("ln1b", [NCH, 128])
        wgn_in = cx.din("wgn", [D, D])
        o_xni = cx.dout("xni", [D, TOK], BF16)
        o_swn = cx.dout("swishn", [D, TOK])
    o_xn = cx.dout("xnew", [D, TOK])
    if DEBUG_L3B:
        d_hcs = cx.dout("d_hcs", [128, TOK])
        d_g0 = cx.dout("d_g0", [128, TOK], BF16)
    with cx.es:
        cx.es.enter_context(nc.Block())
        v, s, t, g, sy = nc.vector, nc.scalar, nc.tensor, nc.gpsimd, nc.sync
        ws = [cx.sb(f"ws{i}", [128, 4096], BF16) for i in range(NCH)]
        xn2b = [cx.sb(f"xn2b{i}", [128, GRP], BF16) for i in range(NCH)]
        x1g = [cx.sb(f"x1g{i}", [128, GRP], F32) for i in range(NCH)]
        xng = x1g
        ghb = [cx.sb(f"ghb{i}", [128, GRP], BF16) for i in range(NH)]
        hsq = cx.sb("hsq", [128, GRP], F32)
        hcs = cx.sb("hcs", [128, GRP], F32)
        t2b = cx.sb("t2b", [128, GRP], F32)
        t3b = cx.sb("t3b", [128, GRP], F32)
        t4b = cx.sb("t4b", [128, GRP], F32)
        t5b = cx.sb("t5b", [128, GRP], F32)
        hsq2 = cx.sb("hsq2", [128, GRP], F32)
        hcs2 = cx.sb("hcs2", [128, GRP], F32)
        t2b2 = cx.sb("t2b2", [128, GRP], F32)
        t3b2 = cx.sb("t3b2", [128, GRP], F32)
        t4b2 = cx.sb("t4b2", [128, GRP], F32)
        t5b2 = cx.sb("t5b2", [128, GRP], F32)
        lnS = _ln_scratch(cx, GRP, "lnS")
        ph = cx.ps("ph", [128, GRP], F32)
        pacc = [cx.ps(f"pacc{j}", [128, GRP], F32) for j in range(2)]
        if not last:
            wgb = [cx.sb(f"wgb{i}", [128, D], BF16) for i in range(NCH)]
            xnib = [cx.sb(f"xnib{i}", [128, GRP], BF16) for i in range(NCH)]
            swg = cx.sb("swg", [128, GRP], F32)
            srow = [cx.sb(f"srow{i}", [1, 128], F32) for i in range(NCH)]
            bcol = [cx.sb(f"bcol{i}", [128, 1], F32) for i in range(NCH)]

        _ln_scratch_init(cx, lnS)
        if not last:
            for i in range(NCH):
                ins = g.dma_start(out=wgb[i][:], in_=wgn_in[128 * i:128 * (i + 1), :])
                cx.inc(ins, "wld", 16)
            for i in range(NCH):
                ins = sy.dma_start(out=srow[i][:], in_=ln1s[i:i + 1, :])
                cx.inc(ins, "cld", 16)
                ins = sy.dma_start(out=bcol[i][:], in_=ln1b[i:i + 1, :].rearrange("a k -> (a k)")[:, None])
                cx.inc(ins, "cld", 16)
            cx.wait(v, "cld")
            for i in range(NCH):
                ins = v.tensor_scalar_mul(lnS["lhs2"][0:1, 128 * i:128 * (i + 1)], srow[i][:], -1.0)
            cx.inc(ins, "negs")
            negsrow = [lnS["lhs2"][0:1, 128 * i:128 * (i + 1)] for i in range(NCH)]

        nw = 0
        for gi in range(NG):
            go = gi * GRP
            pre = f"g{gi}"
            for i in range(NCH):
                ins = sy.dma_start(out=xn2b[i][:], in_=xn2_in[128 * i:128 * (i + 1), go:go + GRP])
                cx.inc(ins, "ld", 16)
                ins = sy.dma_start(out=x1g[i][:], in_=x1_in[128 * i:128 * (i + 1), go:go + GRP])
                cx.inc(ins, "ld", 16)
            for i in range(NCH):
                ins = g.dma_start(out=ws[i][:], in_=w1_in[128 * i:128 * (i + 1), :])
                cx.inc(ins, "wl1", 16)
            cx.wait(t, "wl1")
            cx.wait(t, "ld")
            for hc in range(NH):
                par = hc % 2
                phx = ph if par == 0 else pacc[0]
                hsqx, hcsx, t2x, t3x, t4x, t5x = (
                    (hsq, hcs, t2b, t3b, t4b, t5b) if par == 0 else
                    (hsq2, hcs2, t2b2, t3b2, t4b2, t5b2))
                if hc >= 2:
                    cx.wait(t, pre + "hrd", hc - 1)
                for i in range(NCH):
                    ins = t.matmul(phx[:], ws[i][:, 128 * hc:128 * (hc + 1)], xn2b[i][:],
                                   start=(i == 0), stop=(i == NCH - 1))
                cx.inc(ins, pre + "h1")
                cx.wait(s, pre + "h1", hc + 1)
                if hc >= 2:
                    cx.wait(s, pre + "gd", hc - 1)
                s.activation(out=hsqx[:], in_=phx[:], func=AF.Square)
                ins = s.activation(out=hcsx[:], in_=phx[:], func=AF.Copy, scale=GELU_C)
                cx.inc(ins, pre + "hrd")
                cx.wait(v, pre + "hrd", hc + 1)
                ins = v.tensor_scalar(t2x[:], hsqx[:], 0.044715, 1.0, ALU.mult, ALU.add)
                cx.inc(ins, "_vv")
                cx.wait(v, "_vv")
                ins = v.tensor_mul(t3x[:], t2x[:], hcsx[:])
                cx.inc(ins, pre + "t3")
                cx.wait(s, pre + "t3", hc + 1)
                ins = s.activation(out=t4x[:], in_=t3x[:], func=AF.Tanh)
                cx.inc(ins, pre + "t4")
                cx.wait(v, pre + "t4", hc + 1)
                ins = v.tensor_scalar(t5x[:], t4x[:], GELU_K2, GELU_K2, ALU.mult, ALU.add)
                cx.inc(ins, "_vv")
                cx.wait(v, "_vv")
                ins = v.tensor_mul(ghb[hc][:], t5x[:], hcsx[:])
                cx.inc(ins, pre + "gd")
                if DEBUG_L3B and hc == 0:
                    cx.wait(sy, pre + "gd", 1)
                    ins = sy.dma_start(out=d_hcs[:, go:go + GRP], in_=hcs[:])
                    cx.inc(ins, "st", 16)
                    ins = sy.dma_start(out=d_g0[:, go:go + GRP], in_=ghb[0][:])
                    cx.inc(ins, "st", 16)
            cx.wait(g, pre + "h1", NH)
            for hc in range(NH):
                ins = g.dma_start(out=ws[hc // 4][:, 1024 * (hc % 4):1024 * (hc % 4 + 1)],
                                  in_=w2_in[128 * hc:128 * (hc + 1), :])
                cx.inc(ins, "wl2", 16)
            cx.wait(t, "wl2")
            cx.wait(t, pre + "gd", NH)
            for do in range(NCH):
                pb = pacc[do % 2]
                if do >= 2:
                    cx.wait(t, pre + "xn", do - 1)
                for hc in range(NH):
                    ins = t.matmul(pb[:], ws[hc // 4][:, 1024 * (hc % 4) + 128 * do:1024 * (hc % 4) + 128 * (do + 1)],
                                   ghb[hc][:], start=(hc == 0), stop=(hc == NH - 1))
                cx.inc(ins, pre + "f2")
                cx.wait(v, pre + "f2", do + 1)
                ins = v.tensor_add(x1g[do][:], x1g[do][:], pb[:])
                cx.inc(ins, pre + "xn")
            for i in range(NCH):
                cx.wait(sy, pre + "xn", i + 1)
                ins = sy.dma_start(out=o_xn[128 * i:128 * (i + 1), go:go + GRP], in_=xng[i][:])
                cx.inc(ins, "st", 16)
            if not last:
                apN = emit_fm_norm(cx, [xng[i][:] for i in range(NCH)], [xnib[i][:] for i in range(NCH)],
                             NCH, GRP, EPS1,
                             dict(lnS, srow=[r[:] for r in srow], bcol=[b[:] for b in bcol],
                                  negsrow=negsrow),
                             pre + "n",
                             [(pre + "xn", NCH), ("negs", cx.cnt("negs")), ("cld", cx.cnt("cld")),
                              ("consts", cx.cnt("consts"))])
                for i in range(NCH):
                    cx.wait(sy, apN[0], apN[1] + i + 1)
                    ins = sy.dma_start(out=o_xni[128 * i:128 * (i + 1), go:go + GRP], in_=xnib[i][:])
                    cx.inc(ins, "st", 16)
                cx.wait(t, "wld")
                cx.wait(t, apN[0], apN[1] + NCH)
                for n in range(NCH):
                    if n > 0:
                        cx.wait(t, pre + "sw", n)
                    for i in range(NCH):
                        ins = t.matmul(ph[:], wgb[i][:, 128 * n:128 * (n + 1)], xnib[i][:],
                                       start=(i == 0), stop=(i == NCH - 1))
                    cx.inc(ins, pre + "gmm")
                    cx.wait(s, pre + "gmm", n + 1)
                    if n > 0:
                        cx.wait(s, "st", cx.cnt("st"))
                    ins = s.activation(out=swg[:], in_=ph[:], func=AF.Silu)
                    cx.inc(ins, pre + "sw")
                    cx.wait(sy, pre + "sw", n + 1)
                    ins = sy.dma_start(out=o_swn[128 * n:128 * (n + 1), go:go + GRP], in_=swg[:])
                    cx.inc(ins, "st", 16)
            for e in (v, s, t, g, sy):
                cx.wait(e, "st")
        cx.wait(sy, "st")
    return nc


def host_L3a(inputs, layer, y_by_batch, sw_fm, xp_fm, run=run_bass_kernel_spmd):
    nc = build_L3a()
    in_maps = []
    for c in range(NCORE):
        b, blk = c // 4, c % 4
        in_maps.append({
            "y_fm": np.ascontiguousarray(y_by_batch[b][:, blk * SBLK:(blk + 1) * SBLK]),
            "sw_fm": sw_fm[c],
            "xp_fm": xp_fm[c],
            "wo": np.ascontiguousarray(np.asarray(inputs["L0_attn_Wo"][layer], np.float32)),
            "ln2s": _rows(inputs["L0_attn_ln2_scale"][layer]),
            "ln2b": _rows(inputs["L0_attn_ln2_bias"][layer]),
        })
    res = run(nc, in_maps, core_ids=list(range(NCORE)))
    return res.results


def host_L3b(inputs, layer, xn2, x1, run=run_bass_kernel_spmd):
    last = (layer == 1)
    nc = build_L3b(last)
    in_maps = []
    for c in range(NCORE):
        m = {
            "xn2": xn2[c],
            "x1": x1[c],
            "w1": np.ascontiguousarray(np.asarray(inputs["L0_attn_ffn_w1"][layer], np.float32)),
            "w2": np.ascontiguousarray(np.asarray(inputs["L0_attn_ffn_w2"][layer], np.float32)),
        }
        if not last:
            m["ln1s"] = _rows(inputs["L0_attn_ln1_scale"][layer + 1])
            m["ln1b"] = _rows(inputs["L0_attn_ln1_bias"][layer + 1])
            m["wgn"] = np.ascontiguousarray(np.asarray(inputs["L0_attn_Wg"][layer + 1], np.float32))
        in_maps.append(m)
    res = run(nc, in_maps, core_ids=list(range(NCORE)))
    return res.results





def build_L4():
    cx = Ctx("L4")
    nc = cx.nc
    E, HH = 32, 512
    SUPE, NSUP = 2, 16
    NTT = TOK // 128
    NG4 = TOK // 512
    xn_in = cx.din("xn0", [D, TOK])
    x_in = cx.din("x_fm", [D, TOK])
    x2_in = cx.din("x2", [D, TOK])
    wgr_in = cx.din("wgr", [D, E])
    w1_in = cx.din("w1e", [E, D, HH])
    w2_in = cx.din("w2e", [E, HH, D])
    w3_in = cx.din("w3e", [E, D, HH])
    ws1_in = cx.din("ws1", [D, 2 * HH])
    ws2_in = cx.din("ws2", [2 * HH, D])
    ws3_in = cx.din("ws3", [D, 2 * HH])
    id_in = cx.din("ident", [128, 128])
    er_in = cx.din("erow", [E, E * 128])
    o_xmid = cx.dout("xmid", [D, TOK])
    o_xn1 = cx.dout("xn1", [D, TOK], BF16)
    with cx.es:
        cx.es.enter_context(nc.Block())
        v, s, t, g, sy = nc.vector, nc.scalar, nc.tensor, nc.gpsimd, nc.sync
        xnbf = [cx.sb(f"xnbf{i}", [128, TOK], BF16) for i in range(NCH)]
        yacc = [cx.sb(f"yacc{i}", [128, TOK], F32) for i in range(NCH)]
        wgrb = cx.sb("wgrb", [128, NCH * E], BF16)
        wT = cx.sb("wT", [32, TOK], BF16)
        identb = cx.sb("identb", [128, 128], F32)
        erowb = cx.sb("erowb", [32, E * 128], BF16)
        w1s = [cx.sb(f"w1s{i}", [128, SUPE * HH], BF16) for i in range(NCH)]
        w3s = [cx.sb(f"w3s{i}", [128, SUPE * HH], BF16) for i in range(NCH)]
        w2s = [cx.sb(f"w2s{i}", [128, SUPE * D], BF16) for i in range(4)]
        gwb = [cx.sb(f"gwb{k}", [128, 4 * 512], BF16) for k in range(SUPE)]
        s1b = cx.sb("s1b", [128, 512], BF16)
        s1c = cx.sb("s1c", [128, 512], BF16)
        h3b = cx.sb("h3b", [128, 512], BF16)
        h3c = cx.sb("h3c", [128, 512], BF16)
        wbc = cx.sb("wbc", [128, 512], BF16)
        ra = cx.sb("ra", [128, 32], F32)
        rb = cx.sb("rb", [128, 32], F32)
        rsc = cx.sb("rsc", [128, 32], F32)
        rex = cx.sb("rex", [128, 32], F32)
        rcol = cx.sb("rcol", [128, 1], F32)
        rcol2 = cx.sb("rcol2", [128, 1], F32)
        rsum = cx.sb("rsum", [128, 1], F32)
        xg = cx.sb("xg", [128, 512], F32)
        x2g = cx.sb("x2g", [128, 512], F32)
        xt1 = cx.sb("xt1", [128, 512], F32)
        xn1b = [cx.sb(f"xn1b{i}", [128, 512], BF16) for i in range(NCH)]
        pH1 = cx.ps("pH1", [128, 512], F32)
        pH2 = cx.ps("pH2", [128, 512], F32)
        pW = cx.ps("pW", [128, 512], F32)
        pacc = [cx.ps(f"pacc{j}", [128, 512], F32) for j in range(2)]
        pTr = cx.ps("pTr", [32, 128], F32)
        lnS = _ln_scratch(cx, 512, "lnS", ps_map={"pt1": pH1, "pt2": pH2,
                                                  "psx": cx.ps("lnpsx", [1, 512], F32),
                                                  "pss": cx.ps("lnpss", [1, 512], F32)},
                          need_scale=False)
        lnS["srow"] = None

        _ln_scratch_init(cx, lnS)
        for i in range(NCH):
            ins = g.dma_start(out=xnbf[i][:], in_=xn_in[128 * i:128 * (i + 1), :])
            cx.inc(ins, "xld", 16)
        for i in range(NCH):
            ins = g.dma_start(out=wgrb[:, E * i:E * (i + 1)], in_=wgr_in[128 * i:128 * (i + 1), :])
            cx.inc(ins, "wld", 16)
        ins = sy.dma_start(out=identb[:], in_=id_in[:])
        cx.inc(ins, "cld", 16)
        ins = g.dma_start(out=erowb[:], in_=er_in[:])
        cx.inc(ins, "cld", 16)

        cx.wait(t, "xld")
        cx.wait(t, "wld")
        cx.wait(t, "cld")
        for tt in range(NTT):
            if tt > 0:
                cx.wait(t, "rexp", tt)
            for i in range(NCH):
                ins = t.matmul(pW[:, 0:E], xnbf[i][:, 128 * tt:128 * (tt + 1)],
                               wgrb[:, E * i:E * (i + 1)], start=(i == 0), stop=(i == NCH - 1))
            cx.inc(ins, "rmm")
            cx.wait(v, "rmm", tt + 1)
            ins = v.reduce_max(rcol[:], pW[:, 0:E], axis=AX.X)
            cx.inc(ins, "_vv")
            cx.wait(v, "_vv")
            ins = v.tensor_scalar_mul(rcol2[:], rcol[:], -1.0)
            cx.inc(ins, "rng")
            cx.wait(s, "rng", tt + 1)
            ins = s.activation(out=rex[:], in_=pW[:, 0:E], func=AF.Exp, bias=rcol2[:],
                               accum_out=rsum[:])
            cx.inc(ins, "rexp")
            cx.wait(v, "rexp", tt + 1)
            ins = v.reciprocal(rcol[:], rsum[:])
            cx.inc(ins, "_vv")
            cx.wait(v, "_vv")
            ins = v.tensor_scalar_mul(rsc[:], rex[:], rcol[:])
            cx.inc(ins, "_vv")
            cx.wait(v, "_vv")
            ins = v.tensor_copy(ra[:], rsc[:])
            cx.inc(ins, "_vv")
            cx.wait(v, "_vv")
            cur, nxt = ra, rb
            for r in range(4):
                ins = v.reduce_max(rcol[:], cur[:], axis=AX.X)
                cx.inc(ins, "_vv")
                cx.wait(v, "_vv")
                if r < 3:
                    ins = v.tensor_scalar(rex[:], cur[:], rcol[:], -2.0, ALU.is_ge, ALU.mult)
                    cx.inc(ins, "_vv")
                    cx.wait(v, "_vv")
                    ins = v.tensor_add(nxt[:], cur[:], rex[:])
                    cx.inc(ins, "_vv")
                    cx.wait(v, "_vv")
                    cur, nxt = nxt, cur
            wout = (rb if cur is ra else ra)
            ins = v.scalar_tensor_tensor(wout[:], rsc[:], rcol[:], rsc[:],
                                         ALU.is_ge, ALU.mult)
            cx.inc(ins, "wtm")
            cx.wait(t, "wtm", tt + 1)
            ins = t.transpose(pTr[:], wout[:], identb[:])
            cx.inc(ins, "rtr")
            cx.wait(v, "rtr", tt + 1)
            ins = v.tensor_copy(wT[:, 128 * tt:128 * (tt + 1)], pTr[:])
            cx.inc(ins, "wTd")
            cx.wait(t, "wTd")
        nacc = 0
        for sup in range(NSUP):
            if sup > 0:
                cx.wait(g, "amm", cx.cnt("amm"))
                cx.wait(g, "hmm", cx.cnt("hmm"))
            for le in range(SUPE):
                e = SUPE * sup + le
                for i in range(NCH):
                    ins = g.dma_start(out=w1s[i][:, le * HH:(le + 1) * HH],
                                      in_=w1_in[e, 128 * i:128 * (i + 1), :])
                    cx.inc(ins, "ew", 16)
                    ins = g.dma_start(out=w3s[i][:, le * HH:(le + 1) * HH],
                                      in_=w3_in[e, 128 * i:128 * (i + 1), :])
                    cx.inc(ins, "ew", 16)
                for hcs_i in range(4):
                    ins = g.dma_start(out=w2s[hcs_i][:, le * D:(le + 1) * D],
                                      in_=w2_in[e, 128 * hcs_i:128 * (hcs_i + 1), :])
                    cx.inc(ins, "ew", 16)
            cx.wait(t, "ew")
            for gi in range(NG4):
                go = gi * 512
                kk4 = 0
                sil40 = cx.cnt("sil")
                gwd40 = cx.cnt("gwd")
                for le in range(SUPE):
                    e = SUPE * sup + le
                    if cx.cnt("wbcd") > 0:
                        cx.wait(t, "wbcd", cx.cnt("wbcd"))
                    cx.wait(t, "wTd", NTT)
                    ins = t.matmul(pW[:], erowb[:, 128 * e:128 * (e + 1)], wT[0:32, go:go + 512],
                                   start=True, stop=True)
                    cx.inc(ins, "wbmm")
                    cx.wait(s, "wbmm", cx.cnt("wbmm"))
                    cx.wait(s, "gwd", cx.cnt("gwd"))
                    ins = s.copy(wbc[:], pW[:])
                    cx.inc(ins, "wbcd")
                    for hc in range(4):
                        pu = pH1 if kk4 % 2 == 0 else pacc[0]
                        pgg = pH2 if kk4 % 2 == 0 else pacc[1]
                        sx = s1b if kk4 % 2 == 0 else s1c
                        hx = h3b if kk4 % 2 == 0 else h3c
                        if kk4 >= 2:
                            cx.wait(t, "gwd", gwd40 + kk4 - 1)
                            cx.wait(t, "sil", sil40 + kk4 - 1)
                        for i in range(NCH):
                            t.matmul(pu[:], w1s[i][:, le * HH + 128 * hc:le * HH + 128 * (hc + 1)],
                                     xnbf[i][:, go:go + 512], start=(i == 0), stop=(i == NCH - 1))
                        for i in range(NCH):
                            ins = t.matmul(pgg[:], w3s[i][:, le * HH + 128 * hc:le * HH + 128 * (hc + 1)],
                                           xnbf[i][:, go:go + 512], start=(i == 0), stop=(i == NCH - 1))
                        cx.inc(ins, "hmm")
                        cx.wait(s, "hmm", cx.cnt("hmm"))
                        cx.wait(s, "gwd", max(0, cx.cnt("gwd") - 1))
                        ins = s.activation(out=sx[:], in_=pu[:], func=AF.Silu)
                        cx.inc(ins, "sil")
                        cx.wait(v, "sil", cx.cnt("sil"))
                        cx.wait(v, "wbcd", cx.cnt("wbcd"))
                        ins = v.tensor_mul(hx[:], pgg[:], wbc[:])
                        cx.inc(ins, "_vv")
                        cx.wait(v, "_vv")
                        ins = v.tensor_mul(gwb[le][:, 512 * hc:512 * (hc + 1)], sx[:], hx[:])
                        cx.inc(ins, "gwd")
                        kk4 += 1
                for do in range(NCH):
                    pb = pacc[do % 2]
                    if nacc >= 2:
                        cx.wait(t, "yad", nacc - 1)
                    cx.wait(t, "gwd", cx.cnt("gwd"))
                    for le in range(SUPE):
                        for hc in range(4):
                            ins = t.matmul(pb[:], w2s[hc][:, le * D + 128 * do:le * D + 128 * (do + 1)],
                                           gwb[le][:, 512 * hc:512 * (hc + 1)],
                                           start=(le == 0 and hc == 0), stop=(le == SUPE - 1 and hc == 3))
                    cx.inc(ins, "amm")
                    cx.wait(v, "amm", cx.cnt("amm"))
                    if sup == 0:
                        ins = v.tensor_copy(yacc[do][:, go:go + 512], pb[:])
                    else:
                        cx.wait(v, "yad", cx.cnt("yad"))
                        ins = v.tensor_add(yacc[do][:, go:go + 512], yacc[do][:, go:go + 512], pb[:])
                    cx.inc(ins, "yad")
                    nacc += 1
        cx.wait(g, "amm", cx.cnt("amm"))
        cx.wait(g, "hmm", cx.cnt("hmm"))
        for i in range(NCH):
            ins = g.dma_start(out=w1s[i][:, 0:HH], in_=ws1_in[128 * i:128 * (i + 1), 0:HH])
            cx.inc(ins, "sw", 16)
            ins = g.dma_start(out=w1s[i][:, HH:2 * HH], in_=ws1_in[128 * i:128 * (i + 1), HH:2 * HH])
            cx.inc(ins, "sw", 16)
            ins = g.dma_start(out=w3s[i][:, 0:HH], in_=ws3_in[128 * i:128 * (i + 1), 0:HH])
            cx.inc(ins, "sw", 16)
            ins = g.dma_start(out=w3s[i][:, HH:2 * HH], in_=ws3_in[128 * i:128 * (i + 1), HH:2 * HH])
            cx.inc(ins, "sw", 16)
        for hcs_i in range(8):
            ins = g.dma_start(out=w2s[hcs_i % 4][:, (hcs_i // 4) * D:(hcs_i // 4 + 1) * D],
                              in_=ws2_in[128 * hcs_i:128 * (hcs_i + 1), :])
            cx.inc(ins, "sw", 16)
        for gi in range(NG4):
            go = gi * 512
            cx.wait(t, "sw")
            for hc in range(8):
                cx.wait(t, "gwd", cx.cnt("gwd"))
                cx.wait(t, "sil", cx.cnt("sil"))
                for i in range(NCH):
                    t.matmul(pH1[:], w1s[i][:, 128 * hc:128 * (hc + 1)],
                             xnbf[i][:, go:go + 512], start=(i == 0), stop=(i == NCH - 1))
                for i in range(NCH):
                    ins = t.matmul(pH2[:], w3s[i][:, 128 * hc:128 * (hc + 1)],
                                   xnbf[i][:, go:go + 512], start=(i == 0), stop=(i == NCH - 1))
                cx.inc(ins, "hmm")
                cx.wait(s, "hmm", cx.cnt("hmm"))
                cx.wait(s, "gwd", max(0, cx.cnt("gwd") - 1))
                ins = s.activation(out=s1b[:], in_=pH1[:], func=AF.Silu)
                cx.inc(ins, "sil")
                cx.wait(v, "sil", cx.cnt("sil"))
                ins = v.tensor_mul(h3b[:], pH2[:], s1b[:])
                cx.inc(ins, "_vv")
                cx.wait(v, "_vv")
                ins = v.tensor_copy(gwb[hc // 4][:, 512 * (hc % 4):512 * (hc % 4 + 1)], h3b[:])
                cx.inc(ins, "gwd")
            for do in range(NCH):
                pb = pacc[do % 2]
                if nacc >= 2:
                    cx.wait(t, "yad", nacc - 1)
                cx.wait(t, "gwd", cx.cnt("gwd"))
                for hc in range(8):
                    ins = t.matmul(pb[:], w2s[hc % 4][:, (hc // 4) * D + 128 * do:(hc // 4) * D + 128 * (do + 1)],
                                   gwb[hc // 4][:, 512 * (hc % 4):512 * (hc % 4 + 1)],
                                   start=(hc == 0), stop=(hc == 7))
                cx.inc(ins, "amm")
                cx.wait(v, "amm", cx.cnt("amm"))
                cx.wait(v, "yad", cx.cnt("yad"))
                ins = v.tensor_add(yacc[do][:, go:go + 512], yacc[do][:, go:go + 512], pb[:])
                cx.inc(ins, "yad")
                nacc += 1
        for gi in range(NG4):
            go = gi * 512
            pre = f"c{gi}"
            for i in range(NCH):
                ins = sy.dma_start(out=xg[:], in_=x_in[128 * i:128 * (i + 1), go:go + 512])
                cx.inc(ins, "ld", 16)
                ins = sy.dma_start(out=x2g[:], in_=x2_in[128 * i:128 * (i + 1), go:go + 512])
                cx.inc(ins, "ld", 16)
                cx.wait(v, "ld")
                cx.wait(v, "yad", cx.cnt("yad"))
                ins = v.tensor_add(xt1[:], xg[:], x2g[:])
                cx.inc(ins, "_vv")
                cx.wait(v, "_vv")
                ins = v.tensor_add(yacc[i][:, go:go + 512], xt1[:], yacc[i][:, go:go + 512])
                cx.inc(ins, pre + "xm")
                cx.wait(sy, pre + "xm")
                ins = sy.dma_start(out=o_xmid[128 * i:128 * (i + 1), go:go + 512], in_=yacc[i][:, go:go + 512])
                cx.inc(ins, "st", 16)
            apN = emit_fm_norm(cx, [yacc[i][:, go:go + 512] for i in range(NCH)],
                         [xn1b[i][:] for i in range(NCH)],
                         NCH, 512, EPS0, dict(lnS, srow=None), pre + "n",
                         [(pre + "xm", NCH), ("consts", cx.cnt("consts"))])
            for i in range(NCH):
                cx.wait(sy, apN[0], apN[1] + i + 1)
                ins = sy.dma_start(out=o_xn1[128 * i:128 * (i + 1), go:go + 512], in_=xn1b[i][:])
                cx.inc(ins, "st", 16)
            for e2 in (v, s, t, sy):
                cx.wait(e2, "st")
        cx.wait(sy, "st")
    return nc


def host_L4(inputs, xn0, x_fm, x2, run=run_bass_kernel_spmd):
    nc = build_L4()
    in_maps = []
    for c in range(NCORE):
        in_maps.append({
            "xn0": xn0[c], "x_fm": x_fm[c], "x2": x2[c],
            "wgr": np.ascontiguousarray(np.asarray(inputs["L0_ffn_Wg"], np.float32)),
            "w1e": np.ascontiguousarray(np.asarray(inputs["L0_ffn_W1"], np.float32)),
            "w2e": np.ascontiguousarray(np.asarray(inputs["L0_ffn_W2"], np.float32)),
            "w3e": np.ascontiguousarray(np.asarray(inputs["L0_ffn_W3"], np.float32)),
            "ws1": np.ascontiguousarray(np.asarray(inputs["L0_ffn_Ws1"], np.float32)),
            "ws2": np.ascontiguousarray(np.asarray(inputs["L0_ffn_Ws2"], np.float32)),
            "ws3": np.ascontiguousarray(np.asarray(inputs["L0_ffn_Ws3"], np.float32)),
            "ident": np.eye(128, dtype=np.float32),
            "erow": np.kron(np.eye(32, dtype=np.float32), np.ones((1, 128), np.float32)),
        })
    res = run(nc, in_maps, core_ids=list(range(NCORE)))
    return res.results



def rot_pattern_consts():
    j = np.arange(32, dtype=np.float64)
    inv = 1.0 / 10000.0 ** (2 * j / 64.0)
    pos = np.arange(S, dtype=np.float64)
    ang = pos[:, None] * inv[None, :]
    sn, cs = np.sin(ang), np.cos(ang)
    ck = np.empty((S, 64)); sk = np.empty((S, 64))
    ck[:, 0::2] = cs; ck[:, 1::2] = cs
    sk[:, 0::2] = -sn; sk[:, 1::2] = sn
    ck *= 0.125; sk *= 0.125
    nct = S // 128
    cos_tm = np.empty((128, 64 * nct)); sin_tm = np.empty((128, 64 * nct))
    for ci in range(nct):
        cs_c = cs[128 * ci:128 * (ci + 1)]; sn_c = sn[128 * ci:128 * (ci + 1)]
        cos_tm[:, 64 * ci:64 * ci + 64:2] = cs_c
        cos_tm[:, 64 * ci + 1:64 * ci + 64:2] = cs_c
        sin_tm[:, 64 * ci:64 * ci + 64:2] = sn_c
        sin_tm[:, 64 * ci + 1:64 * ci + 64:2] = sn_c
    f = np.float32
    return (np.ascontiguousarray(ck.T, f), np.ascontiguousarray(sk.T, f),
            cos_tm.astype(f), sin_tm.astype(f))


def build_L5():
    cx = Ctx("L5")
    nc = cx.nc
    NCT = S // 128
    NQG = S // 512
    xnb_in = cx.din("xnb", [D, S], BF16)
    w_in = {nm: cx.din(nm, [4, D, 64]) for nm in ("wq", "wk", "wks", "wv", "wkk", "wvk")}
    ckr_in = cx.din("ckr", [64, S])
    skr_in = cx.din("skr", [64, S])
    ctm_in = cx.din("ctm", [128, 64 * NCT])
    stm_in = cx.din("stm", [128, 64 * NCT])
    msk_in = cx.din("dmask", [4, 128, 512])
    tri_in = cx.din("tris", [2, 128, 128])
    cum_in = cx.din("cumtri", [128, 128])
    sd_in = cx.din("seeds", [2, 64, 64])
    id_in = cx.din("ident", [128, 128])
    o_h = cx.dout("heads", [4, 64, S])
    if DEBUG_L5:
        d_qnt = cx.dout("d_qnt", [128, 64 * (S // 128)], BF16)
        d_qre = cx.dout("d_qre", [65, S], BF16)
        d_kre = cx.dout("d_kre", [65, S], BF16)
        d_qf = cx.dout("d_qf", [64, S], BF16)
        d_kkf = cx.dout("d_kkf", [64, S], BF16)
    with cx.es:
        cx.es.enter_context(nc.Block())
        v, s, t, g, sy = nc.vector, nc.scalar, nc.tensor, nc.gpsimd, nc.sync
        ckr = cx.sb("ckr_b", [64, 512], BF16)
        skr = cx.sb("skr_b", [64, 512], BF16)
        ctm = cx.sb("ctm_b", [128, 64 * NCT], BF16)
        stm = cx.sb("stm_b", [128, 64 * NCT], BF16)
        maskb = [cx.sb(f"mask{m}", [128, 512], BF16) for m in range(4)]
        trii = cx.sb("trii", [128, 128], F32)
        tris_ = cx.sb("tris_", [128, 128], F32)
        cumt = cx.sb("cumt", [128, 128], F32)
        seed8 = cx.sb("seed8", [64, 64], BF16)
        id64 = cx.sb("id64", [64, 64], BF16)
        idb = cx.sb("idb", [128, 128], BF16)
        onescol = cx.sb("onescol", [128, 1], F32)
        onesrow64 = cx.sb("onesrow64", [1, 64], F32)
        onesrow128 = cx.sb("onesrow128", [1, 128], F32)
        wsl = {nm: [cx.sb(f"{nm}{p}", [128, NCH * 64], BF16) for p in range(4)]
               for nm in ("wq", "wk", "wks", "wv", "wkk", "wvk")}
        QF = cx.sb("QF", [64, S], BF16)
        KkF = cx.sb("KkF", [64, S], BF16)
        VkF = cx.sb("VkF", [64, S], BF16)
        KkT = cx.sb("KkT", [128, 64 * NCT], BF16)
        VkT = cx.sb("VkT", [128, 64 * NCT], BF16)
        QrE = cx.sb("QrE", [65, S], BF16)
        KrE = cx.sb("KrE", [65, S], BF16)
        VE = cx.sb("VE", [128, 65 * NCT], BF16)
        QnT = cx.sb("QnT", [128, 64 * NCT], BF16)
        xg = [cx.sb(f"xg_{i}", [128, 512], BF16) for i in range(NCH)]
        S_b = cx.sb("S_b", [64, 64], BF16)
        S_Tb = cx.sb("S_Tb", [64, 64], BF16)
        atb = cx.sb("atb", [128, 128], BF16)
        g1b = cx.sb("g1b", [128, 128], F32)
        gtb = cx.sb("gtb", [128, 128], F32)
        gmb = cx.sb("gmb", [128, 128], F32)
        rcol = cx.sb("rcol", [128, 1], F32)
        rcol2 = cx.sb("rcol2", [128, 1], F32)
        tdb = cx.sb("tdb", [128, 64], F32)
        wvb = cx.sb("wvb", [128, 1], F32)
        n2b = cx.sb("n2b", [128, 1], F32)
        nrm = cx.sb("nrm", [128, 1], F32)
        c0b = cx.sb("c0b", [1, 1], F32)
        zerob = cx.sb("zerob", [1, 1], F32)
        qrotb = cx.sb("qrotb", [128, 64], BF16)
        qrf = cx.sb("qrf", [128, 64], F32)
        qrf2 = cx.sb("qrf2", [128, 64], F32)
        scb = [cx.sb(f"scb{j}", [128, 512], BF16) for j in range(2)]
        oat = cx.sb("oat", [64, 512], F32)
        hout = cx.sb("hout", [64, 512], F32)
        sq64 = cx.sb("sq64", [64, 512], F32)
        ca = cx.sb("ca", [64, 512], F32)
        cb = cx.sb("cb", [64, 512], F32)
        qrow = cx.sb("qrow", [1, 512], F32)
        mkrow = cx.sb("mkrow", [1, 512], F32)
        mkmax = cx.sb("mkmax", [1, 1], F32)

        for buf, src_ap in ((ctm, ctm_in), (stm, stm_in)):
            ins = g.dma_start(out=buf[:], in_=src_ap[:])
            cx.inc(ins, "cld", 16)
        for m in range(4):
            ins = g.dma_start(out=maskb[m][:], in_=msk_in[m])
            cx.inc(ins, "cld", 16)
        for buf, src_ap in ((trii, tri_in[0]), (tris_, tri_in[1]), (cumt, cum_in)):
            ins = sy.dma_start(out=buf[:], in_=src_ap)
            cx.inc(ins, "cld", 16)
        ins = g.dma_start(out=seed8[:], in_=sd_in[0])
        cx.inc(ins, "cld", 16)
        ins = g.dma_start(out=id64[:], in_=sd_in[1])
        cx.inc(ins, "cld", 16)
        ins = g.dma_start(out=idb[:], in_=id_in[:])
        cx.inc(ins, "cld", 16)
        for nm in wsl:
            for p in range(4):
                for i in range(NCH):
                    ins = g.dma_start(out=wsl[nm][p][:, 64 * i:64 * (i + 1)],
                                      in_=w_in[nm][p, 128 * i:128 * (i + 1), :])
                    cx.inc(ins, "wld", 16)
        g.memset(onescol[:], 1.0)
        g.memset(onesrow128[:], 1.0)
        ins = g.memset(onesrow64[:], 1.0)
        cx.inc(ins, "consts")
        for ci in range(NCT):
            ins = g.memset(VE[:, 65 * ci + 64:65 * ci + 65], 1.0)
        cx.inc(ins, "consts")
        ins = g.memset(KrE[64:65, :], 1.0)
        cx.inc(ins, "consts")
        ins = g.memset(zerob[:], 0.0)
        cx.inc(ins, "consts")

        PAIR_SEMS = ["pmm", "tmm", "tmv", "cmb", "seed", "sev", "str", "stev", "gmm",
                     "t1mm", "atrd", "g1e", "gtd", "t1rd", "wv", "cum", "c0e", "n2d",
                     "sqd", "rcp", "ymm", "yrd", "sgm", "sup", "qrot", "ttr", "qtr",
                     "ksq", "ksmm", "kst", "mkd", "mqd", "fsc", "fex", "fev", "fm2",
                     "fden", "fbc", "foe", "fho"]
        PB = {}
        for p in range(4):
            pfx = "L5"
            for _nm in PAIR_SEMS:
                PB[_nm] = cx.cnt("L5" + _nm)
            with ExitStack() as pes:
                pps = {nm: pes.enter_context(nc.psum_tensor(f"pp{nm}{p}", [64, 512], F32))
                       for nm in ("wq", "wk", "wks", "wkk", "wvk")}
                ptm = pes.enter_context(nc.psum_tensor(f"ptm{p}", [128, 192], F32))
                for gi in range(NQG):
                    so = gi * 512
                    cx.wait(sy, "L5cmb", cx.cnt("L5cmb"))
                    cx.wait(sy, "L5tmv", cx.cnt("L5tmv"))
                    cx.wait(g, "L5cmb", cx.cnt("L5cmb"))
                    for i in range(NCH):
                        ins = sy.dma_start(out=xg[i][:], in_=xnb_in[128 * i:128 * (i + 1), so:so + 512])
                        cx.inc(ins, "xld", 16)
                    cx.wait(t, "xld")
                    cx.wait(t, "cld")
                    cx.wait(t, "wld")
                    if cx.cnt("L5cmb") > 0:
                        cx.wait(t, "L5cmb", cx.cnt("L5cmb"))
                    if cx.cnt("L5tmv") > 0:
                        cx.wait(t, "L5tmv", cx.cnt("L5tmv"))
                    for nm in ("wq", "wk", "wks", "wkk", "wvk"):
                        for i in range(NCH):
                            ins = t.matmul(pps[nm][:], wsl[nm][p][:, 64 * i:64 * (i + 1)], xg[i][:],
                                           start=(i == 0), stop=(i == NCH - 1))
                    cx.inc(ins, "L5pmm")
                    for tt in range(4):
                        for w_i, nm in enumerate(("wv", "wkk", "wvk")):
                            for i in range(NCH):
                                ins = t.matmul(ptm[:, 64 * w_i:64 * (w_i + 1)],
                                               xg[i][:, 128 * tt:128 * (tt + 1)],
                                               wsl[nm][p][:, 64 * i:64 * (i + 1)],
                                               start=(i == 0), stop=(i == NCH - 1))
                        cx.inc(ins, "L5tmm")
                        ci = 4 * gi + tt
                        cx.wait(s, "L5tmm", cx.cnt("L5tmm"))
                        s.copy(VE[:, 65 * ci:65 * ci + 64], ptm[:, 0:64])
                        s.copy(KkT[:, 64 * ci:64 * (ci + 1)], ptm[:, 64:128])
                        ins = s.copy(VkT[:, 64 * ci:64 * (ci + 1)], ptm[:, 128:192])
                        cx.inc(ins, "L5tmv")
                        cx.wait(t, "L5tmv", cx.cnt("L5tmv"))
                    ins = g.dma_start(out=ckr[:], in_=ckr_in[:, so:so + 512])
                    cx.inc(ins, "L5ckld", 16)
                    ins = g.dma_start(out=skr[:], in_=skr_in[:, so:so + 512])
                    cx.inc(ins, "L5ckld", 16)
                    cx.wait(v, "L5pmm", cx.cnt("L5pmm"))
                    cx.wait(v, "L5ckld", cx.cnt("L5ckld"))
                    ins = v.tensor_copy(QF[:, so:so + 512], pps["wq"][:])
                    cx.inc(ins, "L5cmb")
                    ins = v.tensor_mul(ca[:], pps["wk"][:], ckr[:])
                    cx.inc(ins, "_vv")
                    cx.wait(v, "_vv")
                    ins = v.tensor_mul(cb[:], pps["wks"][:], skr[:])
                    cx.inc(ins, "_vv")
                    cx.wait(v, "_vv")
                    ins = v.tensor_add(KrE[0:64, so:so + 512], ca[:], cb[:])
                    cx.inc(ins, "L5cmb")
                    ins = v.tensor_copy(KkF[:, so:so + 512], pps["wkk"][:])
                    cx.inc(ins, "L5cmb")
                    ins = v.tensor_copy(VkF[:, so:so + 512], pps["wvk"][:])
                    cx.inc(ins, "L5cmb")
                for e2 in (v, s, t, g, sy):
                    cx.wait(e2, "L5cmb", cx.cnt("L5cmb"))
                    cx.wait(e2, "L5tmv", cx.cnt("L5tmv"))
            if L5_MAXPHASE < 2:
                continue
            with ExitStack() as pes:
                pAT = pes.enter_context(nc.psum_tensor(f"pAT{p}", [128, 128], F32))
                pY = pes.enter_context(nc.psum_tensor(f"pY{p}", [128, 64], F32))
                pG1 = pes.enter_context(nc.psum_tensor(f"pG1{p}", [128, 128], F32))
                pG2 = pes.enter_context(nc.psum_tensor(f"pG2{p}", [128, 128], F32))
                pTC = pes.enter_context(nc.psum_tensor(f"pTC{p}", [128, 128], F32))
                pT1 = pTC[:, 0:64]
                pSg = pes.enter_context(nc.psum_tensor(f"pSg{p}", [1, 1], F32))
                pS = pes.enter_context(nc.psum_tensor(f"pS{p}", [64, 64], F32))
                pSTt = pes.enter_context(nc.psum_tensor(f"pSTt{p}", [64, 64], BF16))
                ins = t.matmul(pS[:], seed8[:], id64[:], start=True, stop=False)
                cx.inc(ins, "L5seed")
                for ci in range(NCT):
                    cx.wait(s, "L5sup", cx.cnt("L5sup"))
                    cx.wait(s, "L5seed", cx.cnt("L5seed"))
                    ins = s.copy(S_b[:], pS[:])
                    cx.inc(ins, "L5sev")
                    cx.wait(t, "L5sev", cx.cnt("L5sev"))
                    if ci > 0:
                        cx.wait(t, "L5stev", PB["stev"] + ci)
                    ins = t.transpose(pSTt[:], S_b[:], id64[:])
                    cx.inc(ins, "L5str")
                    cx.wait(s, "L5str", cx.cnt("L5str"))
                    ins = s.copy(S_Tb[:], pSTt[:])
                    cx.inc(ins, "L5stev")
                    if ci > 0:
                        cx.wait(t, "L5atrd", PB["atrd"] + ci)
                    t.matmul(pAT[:], VkF[:, 128 * ci:128 * (ci + 1)], QF[:, 128 * ci:128 * (ci + 1)],
                             start=True, stop=True)
                    if ci > 0:
                        cx.wait(t, "L5g1e", PB["g1e"] + ci)
                        cx.wait(t, "L5gtd", PB["gtd"] + ci)
                    t.matmul(pG1[:], KkF[:, 128 * ci:128 * (ci + 1)], KkF[:, 128 * ci:128 * (ci + 1)],
                             start=True, stop=True)
                    ins = t.matmul(pG2[:], VkF[:, 128 * ci:128 * (ci + 1)], VkF[:, 128 * ci:128 * (ci + 1)],
                                   start=True, stop=True)
                    cx.inc(ins, "L5gmm")
                    if ci > 0:
                        cx.wait(t, "L5t1rd", PB["t1rd"] + ci)
                    ins = t.matmul(pT1, KkF[:, 128 * ci:128 * (ci + 1)], S_b[:],
                                   start=True, stop=True)
                    cx.inc(ins, "L5t1mm")
                    cx.wait(v, "L5gmm", PB["gmm"] + ci + 1)
                    if ci > 0:
                        cx.wait(v, "L5ymm", PB["ymm"] + ci)
                    ins = v.tensor_mul(atb[:], pAT[:], maskb[0][:, 0:128])
                    cx.inc(ins, "L5atrd")
                    cx.wait(s, "L5gmm", PB["gmm"] + ci + 1)
                    if ci > 0:
                        cx.wait(s, "L5gtd", PB["gtd"] + ci)
                    ins = s.copy(g1b[:], pG1[:])
                    cx.inc(ins, "L5g1e")
                    cx.wait(v, "L5g1e", PB["g1e"] + ci + 1)
                    ins = v.tensor_mul(gtb[:], pG2[:], g1b[:])
                    cx.inc(ins, "L5gtd")
                    cx.wait(v, "L5gtd", cx.cnt("L5gtd"))
                    ins = v.tensor_mul(gmb[:], gtb[:], trii[:])
                    cx.inc(ins, "_vv")
                    cx.wait(v, "_vv")
                    ins = v.reduce_sum(rcol[:], gmb[:], axis=AX.X)
                    cx.inc(ins, "_vv")
                    cx.wait(v, "L5t1mm", PB["t1mm"] + ci + 1)
                    ins = v.tensor_mul(tdb[:], pT1, VkT[:, 64 * ci:64 * (ci + 1)])
                    cx.inc(ins, "L5t1rd")
                    cx.wait(v, "L5t1rd", cx.cnt("L5t1rd"))
                    ins = v.reduce_sum(n2b[:], tdb[:], axis=AX.X)
                    cx.inc(ins, "_vv")
                    cx.wait(v, "_vv")
                    if ci > 0:
                        cx.wait(v, "L5sgm", PB["sgm"] + ci)
                    ins = v.scalar_tensor_tensor(wvb[:], n2b[:], 2.0, rcol[:], ALU.mult, ALU.add)
                    cx.inc(ins, "L5wv")
                    if ci > 0:
                        cx.wait(s, "L5sgm", PB["sgm"] + ci)
                        ins = s.copy(c0b[:], pSg[0:1, 0:1])
                        cx.inc(ins, "L5c0e")
                    cx.wait(t, "L5wv", PB["wv"] + ci + 1)
                    if ci > 0:
                        cx.wait(t, "L5n2d", PB["n2d"] + ci)
                        cx.wait(t, "L5c0e", PB["c0e"] + ci)
                    else:
                        cx.wait(t, "consts", cx.cnt("consts"))
                    t.matmul(pTC[:, 64:65], cumt[:], wvb[:], start=True, stop=False)
                    ins = t.matmul(pTC[:, 64:65], onesrow128[:], c0b[:] if ci > 0 else zerob[:],
                                   start=False, stop=True)
                    cx.inc(ins, "L5cum")
                    cx.wait(v, "L5cum", PB["cum"] + ci + 1)
                    ins = v.tensor_single_scalar(n2b[:], pTC[:, 64:65], 4096.0, ALU.add)
                    cx.inc(ins, "L5n2d")
                    cx.wait(s, "L5n2d", PB["n2d"] + ci + 1)
                    ins = s.activation(out=nrm[:], in_=n2b[:], func=AF.Sqrt)
                    cx.inc(ins, "L5sqd")
                    cx.wait(v, "L5sqd", PB["sqd"] + ci + 1)
                    ins = v.tensor_scalar(rcol[:], nrm[:], 1e-8, None, ALU.add)
                    cx.inc(ins, "_vv")
                    cx.wait(v, "_vv")
                    ins = v.reciprocal(rcol2[:], rcol[:])
                    cx.inc(ins, "L5rcp")
                    cx.wait(t, "L5atrd", PB["atrd"] + ci + 1)
                    if ci > 0:
                        cx.wait(t, "L5yrd", PB["yrd"] + ci)
                    t.matmul(pY[:], atb[:], KkT[:, 64 * ci:64 * (ci + 1)], start=True, stop=False)
                    ins = t.matmul(pY[:], QF[:, 128 * ci:128 * (ci + 1)], S_Tb[:], start=False, stop=True)
                    cx.inc(ins, "L5ymm")
                    cx.wait(v, "L5ymm", PB["ymm"] + ci + 1)
                    cx.wait(v, "L5rcp", cx.cnt("L5rcp"))
                    ins = v.tensor_scalar_mul(QnT[:, 64 * ci:64 * (ci + 1)], pY[:], rcol2[:])
                    cx.inc(ins, "L5yrd")
                    cx.wait(t, "L5n2d", PB["n2d"] + ci + 1)
                    ins = t.matmul(pSg[:], wvb[:], onescol[:, 0:1], start=(ci == 0), stop=(ci == NCT - 1))
                    cx.inc(ins, "L5sgm")
                    cx.wait(t, "L5stev", cx.cnt("L5stev"))
                    ins = t.matmul(pS[:], KkT[:, 64 * ci:64 * (ci + 1)], VkT[:, 64 * ci:64 * (ci + 1)],
                                   start=False, stop=(ci == NCT - 1))
                    cx.inc(ins, "L5sup")
                for e2 in (v, s, t, g, sy):
                    cx.wait(e2, "L5yrd", PB["yrd"] + NCT)
                    cx.wait(e2, "L5sup", cx.cnt("L5sup"))
                    cx.wait(e2, "L5sgm", cx.cnt("L5sgm"))
            if L5_MAXPHASE < 3:
                continue
            with ExitStack() as pes:
                pTq = pes.enter_context(nc.psum_tensor(f"pTq{p}", [64, 128], BF16))
                for ci in range(NCT):
                    qn = QnT[:, 64 * ci:64 * (ci + 1)]
                    cth = ctm[:, 64 * ci:64 * (ci + 1)]
                    sth = stm[:, 64 * ci:64 * (ci + 1)]
                    cx.wait(v, "L5qtr", cx.cnt("L5qtr"))
                    ins = v.tensor_mul(qrf[:], qn[:], cth[:])
                    cx.inc(ins, "_vv")
                    ins = v.tensor_mul(qrf2[:], qn[:], sth[:])
                    cx.inc(ins, "_vv")
                    cx.wait(v, "_vv")
                    ins = v.tensor_sub(qrotb[:, 0:64:2], qrf[:, 0:64:2], qrf2[:, 1:64:2])
                    cx.inc(ins, "_vv")
                    ins = v.tensor_add(qrotb[:, 1:64:2], qrf[:, 1:64:2], qrf2[:, 0:64:2])
                    cx.inc(ins, "L5qrot")
                    cx.wait(t, "L5qrot", cx.cnt("L5qrot"))
                    if ci > 0:
                        cx.wait(t, "L5qtr", PB["qtr"] + ci)
                    ins = t.transpose(pTq[:], qrotb[:, 0:64], idb[:])
                    cx.inc(ins, "L5ttr")
                    cx.wait(v, "L5ttr", PB["ttr"] + ci + 1)
                    ins = v.tensor_copy(QrE[0:64, 128 * ci:128 * (ci + 1)], pTq[:])
                    cx.inc(ins, "L5qtr")
                for e2 in (v, s, t):
                    cx.wait(e2, "L5qtr", PB["qtr"] + NCT)
            if L5_MAXPHASE < 4:
                continue
            with ExitStack() as pes:
                pSt = pes.enter_context(nc.psum_tensor(f"pSt{p}", [1, 512], F32))
                pSc0 = pes.enter_context(nc.psum_tensor(f"pSc0{p}", [128, 512], F32))
                pSc1 = pes.enter_context(nc.psum_tensor(f"pSc1{p}", [128, 512], F32))
                pO = pes.enter_context(nc.psum_tensor(f"pO{p}", [65, 512], F32))
                pDen = pes.enter_context(nc.psum_tensor(f"pDen{p}", [64, 512], F32))
                pScs = [pSc0, pSc1]
                for which, srcb in ((0, KrE), (1, QrE)):
                    for gi in range(NQG):
                        so = gi * 512
                        cx.wait(s, "L5ksmm", cx.cnt("L5ksmm"))
                        ins = s.activation(out=sq64[:], in_=srcb[0:64, so:so + 512], func=AF.Square)
                        cx.inc(ins, "L5ksq")
                        cx.wait(t, "L5ksq", cx.cnt("L5ksq"))
                        if cx.cnt("L5kst") > 0:
                            cx.wait(t, "L5kst", cx.cnt("L5kst"))
                        if cx.cnt("L5mqd") > 0:
                            cx.wait(t, "L5mqd", cx.cnt("L5mqd"))
                        ins = t.matmul(pSt[:], onescol[0:64, :], sq64[:], start=True, stop=True)
                        cx.inc(ins, "L5ksmm")
                        cx.wait(s, "L5ksmm", cx.cnt("L5ksmm"))
                        if cx.cnt("L5gmax") > 0:
                            cx.wait(s, "L5gmax", cx.cnt("L5gmax"))
                        if cx.cnt("L5mqd") > 0:
                            cx.wait(s, "L5mqd", cx.cnt("L5mqd"))
                        ins = s.activation(out=mkrow[:], in_=pSt[:], func=AF.Sqrt)
                        cx.inc(ins, "L5kst")
                        if which == 0:
                            cx.wait(v, "L5kst", cx.cnt("L5kst"))
                            if gi == 0:
                                ins = v.reduce_max(mkmax[:], mkrow[:], axis=AX.X)
                            else:
                                ins = v.reduce_max(rcol[0:1, :], mkrow[:], axis=AX.X)
                                cx.inc(ins, "_vv")
                                cx.wait(v, "_vv")
                                ins = v.tensor_max(mkmax[:], mkmax[:], rcol[0:1, :])
                            cx.inc(ins, "L5gmax")
                        else:
                            cx.wait(v, "L5kst", cx.cnt("L5kst"))
                            ins = v.tensor_scalar(qrow[:], mkrow[:], mkmax[:], -1.0, ALU.mult, ALU.mult)
                            cx.inc(ins, "_vv")
                            cx.wait(v, "_vv")
                            ins = v.tensor_copy(QrE[64:65, so:so + 512], qrow[:])
                            cx.inc(ins, "L5mqd")
                if DEBUG_L5 and p == 0:
                    cx.wait(sy, "L5mqd", cx.cnt("L5mqd"))
                    cx.wait(sy, "L5qtr", cx.cnt("L5qtr"))
                    for dbgout, dbgbuf in ((d_qnt, QnT), (d_qre, QrE), (d_kre, KrE),
                                           (d_qf, QF), (d_kkf, KkF)):
                        ins = sy.dma_start(out=dbgout, in_=dbgbuf[:])
                        cx.inc(ins, "st", 16)
                cx.wait(t, "L5mqd", cx.cnt("L5mqd"))
                exp_done = []
                tile_ready = []
                for gi in range(NQG):
                    so = gi * 512
                    Jmax = 4 * gi + 3
                    base_k = len(exp_done)
                    for J in range(Jmax + 2):
                        bi = J % 2
                        if J <= Jmax:
                            k = len(exp_done)
                            if k >= 2:
                                cx.wait(t, *exp_done[k - 2])
                            ins = t.matmul(pScs[bi][:], KrE[:, 128 * J:128 * (J + 1)],
                                           QrE[:, so:so + 512], start=True, stop=True)
                            cx.inc(ins, "L5fsc")
                            mprime = J - 4 * gi
                            cx.wait(s, "L5fsc", cx.cnt("L5fsc"))
                            if cx.cnt("L5fm2") >= 2:
                                cx.wait(s, "L5fm2", cx.cnt("L5fm2") - 1)
                            ins = s.activation(out=scb[bi][:], in_=pScs[bi][:], func=AF.Exp)
                            cx.inc(ins, "L5fex")
                            exp_done.append(("L5fex", cx.cnt("L5fex")))
                            if mprime >= 0:
                                cx.wait(v, "L5fex", cx.cnt("L5fex"))
                                ins = v.tensor_mul(scb[bi][:], scb[bi][:], maskb[mprime][:])
                                cx.inc(ins, "L5fev")
                                tile_ready.append(("L5fev", cx.cnt("L5fev")))
                            else:
                                tile_ready.append(("L5fex", cx.cnt("L5fex")))
                        if J >= 1:
                            Jp = J - 1
                            cx.wait(t, *tile_ready[base_k + Jp])
                            ins = t.matmul(pO[:], VE[:, 65 * Jp:65 * (Jp + 1)], scb[Jp % 2][:],
                                           start=(Jp == 0), stop=(Jp == Jmax))
                            cx.inc(ins, "L5fm2")
                    cx.wait(v, "L5fm2", cx.cnt("L5fm2"))
                    ins = v.reciprocal(qrow[:], pO[64:65, :])
                    cx.inc(ins, "L5fden")
                    cx.wait(t, "L5fden", cx.cnt("L5fden"))
                    ins = t.matmul(pDen[:], onesrow64[:], qrow[:], start=True, stop=True)
                    cx.inc(ins, "L5fbc")
                    cx.wait(s, "L5fm2", cx.cnt("L5fm2"))
                    ins = s.copy(oat[:], pO[0:64, :])
                    cx.inc(ins, "L5foe")
                    cx.wait(v, "L5fbc", cx.cnt("L5fbc"))
                    cx.wait(v, "L5foe", cx.cnt("L5foe"))
                    ins = v.tensor_mul(hout[:], oat[:], pDen[:])
                    cx.inc(ins, "L5fho")
                    cx.wait(sy, "L5fho", cx.cnt("L5fho"))
                    ins = sy.dma_start(out=o_h[p, :, so:so + 512], in_=hout[:])
                    cx.inc(ins, "st", 16)
                    cx.wait(t, "L5foe", cx.cnt("L5foe"))
                    cx.wait(v, "st", cx.cnt("st"))
                for e2 in (v, s, t, g, sy):
                    cx.wait(e2, "st", cx.cnt("st"))
        cx.wait(sy, "st", cx.cnt("st"))
    return nc


def host_L5(inputs, xnb_by_batch, run=run_bass_kernel_spmd):
    nc = build_L5()
    wqkv = np.asarray(inputs["L1_attn_w_qkv"], np.float32)
    wkvkv = np.asarray(inputs["L1_attn_w_kvkv"], np.float32)
    ck, sk, ctmv, stmv = rot_pattern_consts()
    trii = ((np.arange(128)[None, :] <= np.arange(128)[:, None]).astype(np.float32)
            + (np.arange(128)[None, :] < np.arange(128)[:, None]).astype(np.float32))
    tris = (np.arange(128)[None, :] < np.arange(128)[:, None]).astype(np.float32)
    cumt = (np.arange(128)[:, None] <= np.arange(128)[None, :]).astype(np.float32)
    seeds = np.stack([8.0 * np.eye(64, dtype=np.float32), np.eye(64, dtype=np.float32)])
    in_maps = []
    for c in range(NCORE):
        b = c // 4
        hl = [4 * (c % 4) + k for k in range(4)]
        def hsl(w, base, swap=False):
            sl = np.stack([w[:, base + 64 * h:base + 64 * (h + 1)] for h in hl])
            return np.ascontiguousarray(col_swap(sl) if swap else sl)
        in_maps.append({
            "xnb": xnb_by_batch[b],
            "wq": hsl(wqkv, 0), "wk": hsl(wqkv, 1024), "wks": hsl(wqkv, 1024, swap=True),
            "wv": hsl(wqkv, 2048), "wkk": hsl(wkvkv, 0), "wvk": hsl(wkvkv, 1024),
            "ckr": ck, "skr": sk, "ctm": ctmv, "stm": stmv,
            "dmask": diag_masks(), "tris": np.stack([trii, tris]), "cumtri": cumt,
            "seeds": seeds, "ident": np.eye(128, dtype=np.float32),
        })
    res = run(nc, in_maps, core_ids=list(range(NCORE)))
    return res.results



def build_L7():
    cx = Ctx("L7")
    nc = cx.nc
    HID = 4096
    NHC = HID // 128
    NHALF = 2 if TOK >= 1024 else 1
    TOKH = TOK // NHALF
    NGH = TOKH // 512
    NTT = TOK // 128
    heads_in = cx.din("heads_fm", [D, TOK])
    wout_in = cx.din("w_out", [D, D])
    bout_in = cx.din("b_out", [NCH, 128])
    xnb_in = cx.din("xn1b", [D, TOK], BF16)
    xmid_in = cx.din("xmid", [D, TOK])
    wr_in = cx.din("wr", [D, 8])
    wu_in = cx.din("wu", [7, D, HID])
    wg_in = cx.din("wg", [7, D, HID])
    wd_in = cx.din("wd", [7, HID, D])
    wus_in = cx.din("wus", [D, HID])
    wgs_in = cx.din("wgs", [D, HID])
    wds_in = cx.din("wds", [HID, D])
    id_in = cx.din("ident", [128, 128])
    er_in = cx.din("erow8", [8, 8 * 128])
    o_fin = cx.dout("final", [D, TOK])
    attnf = nc.dram_tensor("attnf_scr", [D, TOK], F32, kind="Internal").ap()
    with cx.es:
        cx.es.enter_context(nc.Block())
        v, s, t, g, sy = nc.vector, nc.scalar, nc.tensor, nc.gpsimd, nc.sync
        wT8 = cx.sb("wT8", [8, TOK], BF16)
        identb = cx.sb("identb", [128, 128], F32)
        erowb = cx.sb("erowb", [8, 8 * 128], BF16)
        boutc = [cx.sb(f"boutc{i}", [128, 1], F32) for i in range(NCH)]
        wrb = cx.sb("wrb", [128, NCH * 8], BF16)
        xnbf = [cx.sb(f"xnbf{i}", [128, TOK], BF16) for i in range(NCH)]
        wbuf = [cx.sb(f"wbuf{i}", [128, HID], BF16) for i in range(NCH)]
        upb = [cx.sb(f"upb{i}", [128, TOKH], BF16) for i in range(NHC)]
        facc = [cx.sb(f"facc{i}", [128, TOKH], F32) for i in range(NCH)]
        ag = cx.sb("ag", [128, 512], F32)
        agf = cx.sb("agf", [128, 512], F32)
        s1b = cx.sb("s1b", [128, 512], BF16)
        s1c = cx.sb("s1c", [128, 512], BF16)
        wbc = cx.sb("wbc", [128, 512], BF16)
        ra = cx.sb("ra", [128, 8], F32)
        rcol = cx.sb("rcol", [128, 1], F32)
        rcol2 = cx.sb("rcol2", [128, 1], F32)
        rsum = cx.sb("rsum", [128, 1], F32)
        rw = cx.sb("rw", [128, 8], F32)
        pA = cx.ps("pA", [128, 512], F32)
        pB = cx.ps("pB", [128, 512], F32)
        pC = cx.ps("pC", [128, 512], F32)
        pacc = [cx.ps(f"pacc{j}", [128, 512], F32) for j in range(2)]
        pTr = cx.ps("pTr", [8, 128], F32)
        pCw = cx.ps("pCw", [128, 512], F32)

        ins = sy.dma_start(out=identb[:], in_=id_in[:])
        cx.inc(ins, "cld", 16)
        ins = g.dma_start(out=erowb[:], in_=er_in[:])
        cx.inc(ins, "cld", 16)
        for i in range(NCH):
            ins = g.dma_start(out=wbuf[i][:, 0:D], in_=wout_in[128 * i:128 * (i + 1), :])
            cx.inc(ins, "wld", 16)
            ins = g.dma_start(out=wrb[:, 8 * i:8 * (i + 1)], in_=wr_in[128 * i:128 * (i + 1), :])
            cx.inc(ins, "wld", 16)
            ins = g.dma_start(out=xnbf[i][:], in_=xnb_in[128 * i:128 * (i + 1), :])
            cx.inc(ins, "xld", 16)
            ins = sy.dma_start(out=boutc[i][:], in_=bout_in[i:i + 1, :].rearrange("a k -> (a k)")[:, None])
            cx.inc(ins, "cld", 16)

        cx.wait(t, "xld")
        cx.wait(t, "wld")
        cx.wait(t, "cld")
        for tt in range(NTT):
            if tt > 0:
                cx.wait(t, "rexp", tt)
            for i in range(NCH):
                ins = t.matmul(pA[:, 0:8], xnbf[i][:, 128 * tt:128 * (tt + 1)],
                               wrb[:, 8 * i:8 * (i + 1)], start=(i == 0), stop=(i == NCH - 1))
            cx.inc(ins, "rmm")
            cx.wait(v, "rmm", tt + 1)
            ins = v.reduce_max(rcol[:], pA[:, 0:8], axis=AX.X)
            cx.inc(ins, "_vv")
            cx.wait(v, "_vv")
            ins = v.tensor_scalar_mul(rcol2[:], rcol[:], -1.0)
            cx.inc(ins, "rng")
            cx.wait(s, "rng", tt + 1)
            ins = s.activation(out=ra[:], in_=pA[:, 0:8], func=AF.Exp, bias=rcol2[:],
                               accum_out=rsum[:])
            cx.inc(ins, "rexp")
            cx.wait(v, "rexp", tt + 1)
            ins = v.reciprocal(rcol[:], rsum[:])
            cx.inc(ins, "_vv")
            cx.wait(v, "_vv")
            ins = v.tensor_scalar(rw[:], ra[:], 1.0, None, ALU.is_ge)
            cx.inc(ins, "_vv")
            ins = v.tensor_scalar_mul(rcol2[:], rcol[:], 2.0)
            cx.inc(ins, "_vv")
            cx.wait(v, "_vv")
            ins = v.tensor_scalar_mul(ra[:], rw[:], rcol2[:])
            cx.inc(ins, "rwd")
            cx.wait(t, "rwd", tt + 1)
            if tt > 0:
                cx.wait(t, "wTd", tt)
            ins = t.transpose(pTr[:], ra[:], identb[:])
            cx.inc(ins, "rtr")
            cx.wait(v, "rtr", tt + 1)
            ins = v.tensor_copy(wT8[:, 128 * tt:128 * (tt + 1)], pTr[:])
            cx.inc(ins, "wTd")
        if L7_MAXPHASE < 2:
            cx.wait(sy, "wTd", NTT)
            return nc
        for i in range(NCH):
            for hfi in range(NHALF):
                ins = g.dma_start(out=upb[NHALF * i + hfi][:, :],
                                  in_=heads_in[128 * i:128 * (i + 1), hfi * TOKH:(hfi + 1) * TOKH])
                cx.inc(ins, "hld", 16)
        cx.wait(t, "hld")
        for gi in range(TOK // 512):
            go = gi * 512
            hfi, sub = divmod(go, TOKH)
            for n in range(NCH):
                if cx.cnt("aod") > 0:
                    cx.wait(t, "aod", cx.cnt("aod"))
                for i in range(NCH):
                    ins = t.matmul(pB[:], wbuf[i][:, 128 * n:128 * (n + 1)],
                                   upb[NHALF * i + hfi][:, sub:sub + 512],
                                   start=(i == 0), stop=(i == NCH - 1))
                cx.inc(ins, "homm")
                cx.wait(v, "homm", cx.cnt("homm"))
                cx.wait(v, "aog", cx.cnt("aog"))
                ins = v.tensor_scalar(ag[:], pB[:], boutc[n][:], None, ALU.add)
                cx.inc(ins, "aod")
                cx.wait(g, "aod", cx.cnt("aod"))
                ins = g.dma_start(out=attnf[128 * n:128 * (n + 1), go:go + 512], in_=ag[:])
                cx.inc(ins, "aog", 16)
        for e2 in (v, s, t, g, sy):
            cx.wait(e2, "aog", cx.cnt("aog"))
            cx.wait(e2, "wTd", NTT)
        if L7_MAXPHASE < 3:
            cx.wait(sy, "aog", cx.cnt("aog"))
            return nc
        for hf in range(NHALF):
            ho = hf * TOKH
            for i in range(NCH):
                ins = sy.dma_start(out=facc[i][:, :], in_=xmid_in[128 * i:128 * (i + 1), ho:ho + TOKH])
                cx.inc(ins, "fld", 16)
            for i in range(NCH):
                for sub in range(TOKH // 512):
                    so2 = 512 * sub
                    cx.wait(sy, "fin", cx.cnt("fin"))
                    ins = sy.dma_start(out=agf[:], in_=attnf[128 * i:128 * (i + 1), ho + so2:ho + so2 + 512])
                    cx.inc(ins, "fld", 16)
                    cx.wait(v, "fld")
                    ins = v.tensor_add(facc[i][:, so2:so2 + 512], facc[i][:, so2:so2 + 512], agf[:])
                    cx.inc(ins, "fin")
            for u in range(L7_NU):
                shared = (u == 7)
                cx.wait(g, "upmm", cx.cnt("upmm"))
                cx.wait(g, "ggmm", cx.cnt("ggmm"))
                cx.wait(g, "dmm", cx.cnt("dmm"))
                wlu0 = cx.cnt("wlu")
                for half in range(2):
                    for i in range(NCH):
                        ins = g.dma_start(out=wbuf[i][:, 2048 * half:2048 * (half + 1)],
                                          in_=(wus_in if shared else wu_in[u])[128 * i:128 * (i + 1),
                                                                               2048 * half:2048 * (half + 1)])
                        cx.inc(ins, "wlu", 16)
                upev0 = cx.cnt("upev")
                kk = 0
                for hc in range(NHC):
                    if hc == 0:
                        cx.wait(t, "wlu", wlu0 + 128)
                    if hc == 16:
                        cx.wait(t, "wlu", wlu0 + 256)
                    for gg2 in range(NGH):
                        go2 = ho + gg2 * 512
                        px = pA if kk % 2 == 0 else pC
                        if kk >= 2:
                            cx.wait(t, "upev", upev0 + kk - 1)
                        for i in range(NCH):
                            ins = t.matmul(px[:], wbuf[i][:, 128 * hc:128 * (hc + 1)],
                                           xnbf[i][:, go2:go2 + 512], start=(i == 0), stop=(i == NCH - 1))
                        cx.inc(ins, "upmm")
                        cx.wait(s, "upmm", cx.cnt("upmm"))
                        cx.wait(s, "dmm", cx.cnt("dmm"))
                        ins = s.copy(upb[hc][:, 512 * gg2:512 * (gg2 + 1)], px[:])
                        cx.inc(ins, "upev")
                        kk += 1
                if L7_SUB < 2:
                    continue
                cx.wait(g, "upmm", cx.cnt("upmm"))
                wlg0 = cx.cnt("wlg")
                for half in range(2):
                    for i in range(NCH):
                        ins = g.dma_start(out=wbuf[i][:, 2048 * half:2048 * (half + 1)],
                                          in_=(wgs_in if shared else wg_in[u])[128 * i:128 * (i + 1),
                                                                               2048 * half:2048 * (half + 1)])
                        cx.inc(ins, "wlg", 16)
                cx.wait(t, "upev", cx.cnt("upev"))
                kg = 0
                sil0 = cx.cnt("sil")
                for gg2 in range(NGH):
                    go2 = ho + gg2 * 512
                    if not shared:
                        if cx.cnt("wbcd") > 0:
                            cx.wait(t, "wbcd", cx.cnt("wbcd"))
                        cx.wait(t, "gtd", cx.cnt("gtd"))
                        ins = t.matmul(pCw[:], erowb[:, 128 * u:128 * (u + 1)], wT8[0:8, go2:go2 + 512],
                                       start=True, stop=True)
                        cx.inc(ins, "wbmm")
                        cx.wait(s, "wbmm", cx.cnt("wbmm"))
                        cx.wait(s, "gtd", cx.cnt("gtd"))
                        ins = s.copy(wbc[:], pCw[:])
                        cx.inc(ins, "wbcd")
                    for hc in range(NHC):
                        if hc == 0 and gg2 == 0:
                            cx.wait(t, "wlg", wlg0 + 128)
                        if hc == 16 and gg2 == 0:
                            cx.wait(t, "wlg", wlg0 + 256)
                        px = pB if kg % 2 == 0 else pA
                        sx = s1b if kg % 2 == 0 else s1c
                        if kg >= 2:
                            cx.wait(t, "sil", sil0 + kg - 1)
                        for i in range(NCH):
                            ins = t.matmul(px[:], wbuf[i][:, 128 * hc:128 * (hc + 1)],
                                           xnbf[i][:, go2:go2 + 512], start=(i == 0), stop=(i == NCH - 1))
                        cx.inc(ins, "ggmm")
                        cx.wait(s, "ggmm", cx.cnt("ggmm"))
                        cx.wait(s, "gtd", max(0, cx.cnt("gtd") - 1))
                        ins = s.activation(out=sx[:], in_=px[:], func=AF.Silu)
                        cx.inc(ins, "sil")
                        cx.wait(v, "sil", cx.cnt("sil"))
                        cx.wait(v, "wbcd", cx.cnt("wbcd"))
                        up_sl = upb[hc][:, 512 * gg2:512 * (gg2 + 1)]
                        if not shared:
                            ins = v.tensor_mul(up_sl, up_sl, sx[:])
                            cx.inc(ins, "_vv")
                            cx.wait(v, "_vv")
                            ins = v.tensor_mul(up_sl, up_sl, wbc[:])
                            cx.inc(ins, "gtd")
                        else:
                            ins = v.tensor_mul(up_sl, up_sl, sx[:])
                            cx.inc(ins, "gtd")
                        kg += 1
                if L7_SUB < 3:
                    continue
                cx.wait(g, "ggmm", cx.cnt("ggmm"))
                cx.wait(g, "gtd", cx.cnt("gtd"))
                wldd0 = cx.cnt("wldd")
                for hc in range(NHC):
                    ins = g.dma_start(out=wbuf[hc // 4][:, 1024 * (hc % 4):1024 * (hc % 4 + 1)],
                                      in_=(wds_in if shared else wd_in[u])[128 * hc:128 * (hc + 1), :])
                    cx.inc(ins, "wldd", 16)
                cx.wait(t, "gtd", cx.cnt("gtd"))
                first_dw = True
                for gg2 in range(NGH):
                    for do in range(NCH):
                        pb2 = pacc[do % 2]
                        if cx.cnt("fup") >= 2:
                            cx.wait(t, "fup", cx.cnt("fup") - 1)
                        if first_dw:
                            cx.wait(t, "wldd", wldd0 + 16 * 16)
                        for hc in range(NHC):
                            if first_dw and hc == 16:
                                cx.wait(t, "wldd", wldd0 + 16 * 32)
                            ins = t.matmul(pb2[:], wbuf[hc // 4][:, 1024 * (hc % 4) + 128 * do:1024 * (hc % 4) + 128 * (do + 1)],
                                           upb[hc][:, 512 * gg2:512 * (gg2 + 1)],
                                           start=(hc == 0), stop=(hc == NHC - 1))
                        first_dw = False
                        cx.inc(ins, "dmm")
                        cx.wait(v, "dmm", cx.cnt("dmm"))
                        cx.wait(v, "fup", cx.cnt("fup"))
                        fsl = facc[do][:, 512 * gg2:512 * (gg2 + 1)]
                        ins = v.tensor_add(fsl, fsl, pb2[:])
                        cx.inc(ins, "fup")
            for i in range(NCH):
                cx.wait(sy, "fup", cx.cnt("fup"))
                ins = sy.dma_start(out=o_fin[128 * i:128 * (i + 1), ho:ho + TOKH], in_=facc[i][:, :])
                cx.inc(ins, "st", 16)
            for e2 in (v, s, t, g, sy):
                cx.wait(e2, "st", cx.cnt("st"))
        cx.wait(sy, "st", cx.cnt("st"))
    return nc


def host_L7(inputs, heads_fm, xn1b, xmid, run=run_bass_kernel_spmd):
    nc = build_L7()
    in_maps = []
    for c in range(NCORE):
        in_maps.append({
            "heads_fm": heads_fm[c], "xn1b": xn1b[c], "xmid": xmid[c],
            "w_out": np.ascontiguousarray(np.asarray(inputs["L1_attn_w_out"], np.float32)),
            "b_out": _rows(inputs["L1_attn_b_out"]),
            "wr": np.ascontiguousarray(np.asarray(inputs["L1_ffn_Wr"], np.float32)),
            "wu": np.ascontiguousarray(np.asarray(inputs["L1_ffn_Wu"], np.float32)),
            "wg": np.ascontiguousarray(np.asarray(inputs["L1_ffn_Wg"], np.float32)),
            "wd": np.ascontiguousarray(np.asarray(inputs["L1_ffn_Wd"], np.float32)),
            "wus": np.ascontiguousarray(np.asarray(inputs["L1_ffn_Wu_s"], np.float32)),
            "wgs": np.ascontiguousarray(np.asarray(inputs["L1_ffn_Wg_s"], np.float32)),
            "wds": np.ascontiguousarray(np.asarray(inputs["L1_ffn_Wd_s"], np.float32)),
            "ident": np.eye(128, dtype=np.float32),
            "erow8": np.kron(np.eye(8, dtype=np.float32), np.ones((1, 128), np.float32)),
        })
    res = run(nc, in_maps, core_ids=list(range(NCORE)))
    return res.results


def _asm_batch_fm(parts, rows_per_part=None):
    return [np.ascontiguousarray(np.concatenate([parts[4 * b + k] for k in range(4)], axis=1))
            for b in range(B)]


def kernel(**inputs):
    inputs = {k: np.asarray(v) for k, v in inputs.items()}
    x = inputs["L0_attn_x"]
    x_fm = [_fmslice(x, c) for c in range(NCORE)]

    r1 = host_L1(inputs)
    xn0 = [np.asarray(r1[c]["xn0"]) for c in range(NCORE)]
    xni = [np.asarray(r1[c]["xni0"]) for c in range(NCORE)]
    sw = [np.asarray(r1[c]["swish0"]) for c in range(NCORE)]
    xprev = xn0

    for layer in range(2):
        xnb = _asm_batch_fm(xni)
        r2 = host_L2(inputs, layer, xnb)
        y_by_batch = [np.zeros((D, S), np.float32) for _ in range(B)]
        for c in range(NCORE):
            b = c // 4
            r = np.asarray(r2[c]["y"], np.float32)
            for p in range(2):
                h = 2 * (c % 4) + p
                y_by_batch[b][h * 128:(h + 1) * 128, :] = r[p]
        r3 = host_L3a(inputs, layer, y_by_batch, sw, xprev)
        x1 = [np.asarray(r3[c]["x1"]) for c in range(NCORE)]
        xn2 = [np.asarray(r3[c]["xn2"]) for c in range(NCORE)]
        r4 = host_L3b(inputs, layer, xn2, x1)
        xnew = [np.asarray(r4[c]["xnew"]) for c in range(NCORE)]
        if layer == 0:
            xni = [np.asarray(r4[c]["xni"]) for c in range(NCORE)]
            sw = [np.asarray(r4[c]["swishn"]) for c in range(NCORE)]
            xprev = xnew
    x2 = xnew

    r5 = host_L4(inputs, xn0, x_fm, x2)
    xmid = [np.asarray(r5[c]["xmid"]) for c in range(NCORE)]
    xn1b = [np.asarray(r5[c]["xn1"]) for c in range(NCORE)]

    xnb1 = _asm_batch_fm(xn1b)
    r6 = host_L5(inputs, xnb1)
    heads_by_batch = [np.zeros((D, S), np.float32) for _ in range(B)]
    for c in range(NCORE):
        b = c // 4
        r = np.asarray(r6[c]["heads"], np.float32)
        for p in range(4):
            h = 4 * (c % 4) + p
            heads_by_batch[b][h * 64:(h + 1) * 64, :] = r[p]
    heads_fm = []
    for c in range(NCORE):
        b, blk = c // 4, c % 4
        heads_fm.append(np.ascontiguousarray(heads_by_batch[b][:, blk * SBLK:(blk + 1) * SBLK]))

    r7 = host_L7(inputs, heads_fm, xn1b, xmid)
    out = np.zeros((B, S, D), np.float32)
    for c in range(NCORE):
        b, blk = c // 4, c % 4
        out[b, blk * SBLK:(blk + 1) * SBLK, :] = np.asarray(r7[c]["final"], np.float32).T
    return out





